# version 05
# baseline (speedup 1.0000x reference)
import os
import numpy as np
import concourse.bass as bass
import concourse.mybir as mybir
from concourse.bass_utils import run_bass_kernel_spmd

F32 = mybir.dt.float32
BF16 = mybir.dt.bfloat16
AF = mybir.ActivationFunctionType
ALU = mybir.AluOpType
AX = mybir.AxisListType

NB = 2
N = 2048
D = 1024
NT = N // 128
ND = D // 128
DH = 64
SCALE = DH ** -0.5
CLAMP = 50.0
NEXP = 8
BIG = 1e30
EPS = 1e-5

XN = 0
KF = 16384
VF = 24576
WQT = 32768
WKT = 36864
WVT = 40960
WOT = 45056
QB = 49152
OA = 51200
PP = 53248
GX = 55296
LNS = 57344
AB_COLS = 58368
W1BUF = [KF, WQT]
W2BUF = [VF, WVT]
GBUF = [QB, PP]

XR = 0
TT0 = 16384
TT1 = 17408
SQ = 18432
REC = 19456
CAL = 19968
STAT = 20096
GL = 20224
PEX = 20352
M1 = 20480
T2 = 20608
SGL = 20736
GBB = 20816
GBS = 20824
TMP = SQ
XF_COLS = 20832

OVL = {
    "A": ("A", "Alo", "Ahi"), "Alo": ("A", "Alo"), "Ahi": ("A", "Ahi"),
    "B": ("B", "Blo", "Bhi"), "Blo": ("B", "Blo"), "Bhi": ("B", "Bhi"),
    "Clo": ("Clo",), "Chi": ("Chi",), "D": ("D",), "T": ("T",),
    "LNS": ("LNS",),
}


class Sem:
    __slots__ = ("h", "n", "waited")

    def __init__(self, nc, name):
        self.h = nc.alloc_semaphore(name)
        self.n = 0
        self.waited = {}


class Prog:
    def __init__(self, nc):
        self.nc = nc
        self.st = {k: [] for k in ("sync", "tensor", "vector", "scalar",
                                   "gpsimd")}
        self.free = {}

    def emit(self, eng, fn, inc=None):
        if inc is None:
            self.st[eng].append(fn)
            return None
        sem, amt = inc
        sem.n += amt

        def play(e, fn=fn, h=sem.h, amt=amt):
            fn(e).then_inc(h, amt)

        self.st[eng].append(play)
        return sem.n

    def wait(self, eng, sem, tok):
        if tok is None or tok <= 0:
            return
        if sem.waited.get(eng, 0) >= tok:
            return
        sem.waited[eng] = tok
        self.st[eng].append(lambda e, h=sem.h, t=tok: e.wait_ge(h, t))

    def bank_wait(self, eng, key):
        for k in OVL[key]:
            if k in self.free:
                sem, tok = self.free[k]
                self.wait(eng, sem, tok)

    def bank_set(self, key, sem, tok):
        self.free[key] = (sem, tok)


def build_program():
    nc = bass.Bass("TRN2", target_bir_lowering=False)

    xd = nc.dram_tensor("x", [NB, N, D], F32, kind="ExternalInput")
    wqd = nc.dram_tensor("wqt", [D, D], F32, kind="ExternalInput")
    wkd = nc.dram_tensor("wkt", [D, D], F32, kind="ExternalInput")
    wvd = nc.dram_tensor("wvt", [D, D], F32, kind="ExternalInput")
    wod = nc.dram_tensor("wot", [D, D], F32, kind="ExternalInput")
    gwd = nc.dram_tensor("gwl", [128, 64], F32, kind="ExternalInput")
    gbd = nc.dram_tensor("gbv", [NEXP], F32, kind="ExternalInput")
    sw1d = nc.dram_tensor("sw1", [D, D], F32, kind="ExternalInput")
    sb1d = nc.dram_tensor("sb1l", [128, 8], F32, kind="ExternalInput")
    sw2d = nc.dram_tensor("sw2", [D, D], F32, kind="ExternalInput")
    sb2d = nc.dram_tensor("sb2", [D], F32, kind="ExternalInput")
    w1d = nc.dram_tensor("w1", [NEXP, D, D], F32, kind="ExternalInput")
    b1d = nc.dram_tensor("b1l", [128, 64], F32, kind="ExternalInput")
    w2d = nc.dram_tensor("w2", [NEXP, D, D], F32, kind="ExternalInput")
    b2d = nc.dram_tensor("b2", [NEXP, D], F32, kind="ExternalInput")
    outd = nc.dram_tensor("out", [NB, N, D], F32, kind="ExternalOutput")

    AB = nc.alloc_sbuf_tensor("AB", [128, AB_COLS], BF16)
    XF = nc.alloc_sbuf_tensor("XF", [128, XF_COLS], F32)
    ident = nc.alloc_sbuf_tensor("ident", [128, 128], BF16)
    tri = nc.alloc_sbuf_tensor("tri", [128, 128], BF16)
    ones64 = nc.alloc_sbuf_tensor("ones64", [128, 64], BF16)
    ones1 = nc.alloc_sbuf_tensor("ones1", [1, 128], BF16)
    onesP = nc.alloc_sbuf_tensor("onesP", [128, 128], BF16)
    gbhl = nc.alloc_sbuf_tensor("gbhl", [1, 16], BF16)
    gwS = nc.alloc_sbuf_tensor("gwS", [128, 64], BF16)
    b1f = nc.alloc_sbuf_tensor("b1f", [128, 64], F32)
    sb1f = nc.alloc_sbuf_tensor("sb1f", [128, 8], F32)
    bmat = nc.alloc_sbuf_tensor("bmat", [8, D], BF16)
    bmatS = nc.alloc_sbuf_tensor("bmatS", [1, D], BF16)
    bct = nc.alloc_sbuf_tensor("bct", [8, 2048], BF16)
    cb8 = nc.alloc_sbuf_tensor("cb8", [128, 128], BF16)
    cst = nc.alloc_sbuf_tensor("cst", [128, 4], F32)
    nc.const_aps.register(F32, EPS, cst[:, 0:1])
    nc.const_aps.register(F32, SCALE / CLAMP, cst[:, 1:2])
    nc.const_aps.register(F32, CLAMP, cst[:, 2:3])

    psA = nc.alloc_psum_tensor("psA", [128, 1024], F32)
    psB = nc.alloc_psum_tensor("psB", [128, 1024], F32)
    psC = nc.alloc_psum_tensor("psC", [128, 1024], F32)
    psD = nc.alloc_psum_tensor("psD", [128, 512], F32)
    psT = nc.alloc_psum_tensor("psT", [128, 1024], BF16)

    DBG = os.environ.get("KDBG", "")
    if DBG:
        dbgb = nc.dram_tensor("dbgb", [128, 8192], BF16,
                              kind="ExternalOutput")
        dbgf = nc.dram_tensor("dbgf", [128, 8192], F32,
                              kind="ExternalOutput")

    p = Prog(nc)
    sx = Sem(nc, "sx")
    sc = Sem(nc, "sc")
    swt = [Sem(nc, "swt%d" % i) for i in range(4)]
    swe = [Sem(nc, "swe0"), Sem(nc, "swe1")]
    sw = Sem(nc, "sw")
    so = Sem(nc, "so")
    st = Sem(nc, "st")
    sv = Sem(nc, "sv")
    ss = Sem(nc, "ss")
    sg = Sem(nc, "sg")

    def xb(col, n):
        return bass.AP(XF.ap().tensor, col, [[XF_COLS, 128], [0, n]])

    x_h = xd.ap().rearrange("b (i q) d -> b i q d", q=128)
    out_h = outd.ap().rearrange("b (i q) d -> b i q d", q=128)
    psT_v = psT.ap().rearrange("q (a t) -> q a t", a=ND)

    tapped = [False]

    def tap(name, src_ap, ncols, sem, tok, f32=False):
        if DBG != name or tapped[0]:
            return
        tapped[0] = True
        p.wait("sync", sem, tok)
        dst = (dbgf if f32 else dbgb).ap()[:, 0:ncols]
        tok_d = p.emit("sync", lambda s: s.dma_start(dst, src_ap),
                       inc=(so, 16))
        for eng in ("vector", "scalar", "tensor", "gpsimd"):
            p.wait(eng, so, tok_d)

    def emit_constants():
        p.emit("gpsimd", lambda g: g.memset(onesP[:], 1.0))
        p.emit("gpsimd", lambda g: g.memset(ones64[:], 1.0))
        p.emit("gpsimd", lambda g: g.memset(ones1[:], 1.0))
        p.emit("gpsimd", lambda g: g.affine_select(
            ident[:], onesP[:], [[-1, 128]], ALU.is_equal, 0.0,
            base=0, channel_multiplier=1))
        p.emit("gpsimd", lambda g: g.affine_select(
            tri[:], onesP[:], [[1, 128]], ALU.is_ge, 0.0,
            base=0, channel_multiplier=-1))
        p.emit("gpsimd", lambda g: g.memset(cst[:, 0:1], EPS))
        p.emit("gpsimd", lambda g: g.memset(cst[:, 1:2], SCALE / CLAMP))
        p.emit("gpsimd", lambda g: g.memset(cst[:, 2:3], CLAMP))
        tok_g = p.emit("gpsimd", lambda g: g.memset(cst[:, 3:4], 0.0),
                       inc=(sg, 1))
        p.wait("vector", sg, tok_g)
        sgtok[0] = tok_g
        p.emit("gpsimd", lambda g: g.dma_start(
            gwS[:], gwd.ap()), inc=(sc, 16))
        p.emit("gpsimd", lambda g: g.dma_start(bmat[:], b2d.ap()),
               inc=(sc, 16))
        p.emit("gpsimd", lambda g: g.dma_start(
            bmatS[:], sb2d.ap().rearrange("(a d) -> a d", a=1)),
            inc=(sc, 16))
        p.emit("sync", lambda s: s.dma_start(b1f[:], b1d.ap()),
               inc=(sc, 16))
        p.emit("sync", lambda s: s.dma_start(sb1f[:], sb1d.ap()),
               inc=(sc, 16))
        tok_gb = p.emit("sync", lambda s: s.dma_start(
            XF[0:1, GBS:GBS + 8],
            gbd.ap().rearrange("(a e) -> a e", a=1)), inc=(sc, 16))
        tok_bm = tok_gw = tok_gb
        p.wait("vector", sc, tok_gb)
        p.emit("vector", lambda v: v.tensor_copy(gbhl[0:1, 0:8],
                                                 XF[0:1, GBS:GBS + 8]))
        p.emit("vector", lambda v: v.drain(fusable=False))
        p.emit("vector", lambda v: v.scalar_tensor_tensor(
            XF[0:1, TMP:TMP + 8], gbhl[0:1, 0:8], -1.0,
            XF[0:1, GBS:GBS + 8], ALU.mult, ALU.add))
        p.emit("vector", lambda v: v.drain(fusable=False))
        tok_v = p.emit("vector", lambda v: v.tensor_copy(
            gbhl[0:1, 8:16], XF[0:1, TMP:TMP + 8]), inc=(sv, 1))
        p.wait("tensor", sv, tok_v)
        p.wait("tensor", sg, tok_g)
        p.emit("tensor", lambda t: t.matmul(
            psD[:, 0:8], ones1[:], gbhl[0:1, 0:8], start=True, stop=False))
        tok_t = p.emit("tensor", lambda t: t.matmul(
            psD[:, 0:8], ones1[:], gbhl[0:1, 8:16], start=False, stop=True),
            inc=(st, 1))
        p.wait("vector", st, tok_t)
        tok_v2 = p.emit("vector", lambda v: v.tensor_copy(
            XF[:, GBB:GBB + 8], psD[:, 0:8]), inc=(sv, 1))
        p.bank_set("D", sv, tok_v2)
        return tok_bm, tok_gw

    sgtok = [0]

    def emit_ln(b, xload_toks, pre_sem=None, pre_tok=None):
        xn_ready = [0] * NT
        tok_sq = 0
        for i in range(NT):
            if i == 0:
                if xload_toks is not None:
                    p.wait("vector", sx, xload_toks[NT - 1])
                    p.wait("scalar", sx, xload_toks[NT - 1])
                if pre_tok:
                    p.wait("vector", pre_sem, pre_tok)
                    p.wait("scalar", pre_sem, pre_tok)
            c = STAT + 8 * i
            xt = XF[:, XR + 1024 * i:XR + 1024 * i + 1024]
            p.emit("vector", lambda v, xt=xt, c=c: v.reduce_sum(
                XF[:, c:c + 1], xt, AX.X))
            tok_sq = p.emit("scalar", lambda s, xt=xt, c=c: s.activation(
                XF[:, SQ:SQ + 1024], xt, AF.Square,
                accum_out=XF[:, c + 1:c + 2]), inc=(ss, 1))
        p.wait("vector", ss, tok_sq)
        for i in range(NT):
            c = STAT + 8 * i
            p.emit("vector", lambda v, c=c: v.tensor_scalar_mul(
                XF[:, c + 3:c + 4], XF[:, c + 1:c + 2], 1.0 / D))
        for i in range(NT):
            c = STAT + 8 * i
            p.emit("vector", lambda v, c=c: v.tensor_scalar_mul(
                XF[:, c + 2:c + 3], XF[:, c:c + 1], 1.0 / D))
        for i in range(NT):
            c = STAT + 8 * i
            p.emit("vector", lambda v, c=c: v.tensor_mul(
                XF[:, c + 4:c + 5], XF[:, c + 2:c + 3], XF[:, c + 2:c + 3]))
        tok_var = 0
        for i in range(NT):
            c = STAT + 8 * i
            tok_var = p.emit("vector", lambda v, c=c: v.scalar_tensor_tensor(
                XF[:, c + 5:c + 6], XF[:, c + 4:c + 5], -1.0,
                XF[:, c + 3:c + 4], ALU.mult, ALU.add),
                inc=(sv, 1) if i == NT - 1 else None)
        p.wait("scalar", sv, tok_var)
        p.wait("scalar", sg, sgtok[0])
        tok_sd = 0
        for i in range(NT):
            c = STAT + 8 * i
            tok_sd = p.emit("scalar", lambda s, c=c: s.activation(
                XF[:, c + 6:c + 7], XF[:, c + 5:c + 6], AF.Sqrt, bias=EPS),
                inc=(ss, 1) if i == NT - 1 else None)
        p.wait("vector", ss, tok_sd)
        for i in range(NT):
            c = STAT + 8 * i
            p.emit("vector", lambda v, c=c: v.reciprocal(
                XF[:, c + 7:c + 8], XF[:, c + 6:c + 7]))
        for i in range(NT):
            c = STAT + 8 * i
            xt = XF[:, XR + 1024 * i:XR + 1024 * i + 1024]
            p.bank_wait("vector", "LNS")
            p.emit("vector", lambda v, xt=xt, c=c: v.tensor_tensor(
                XF[:, TMP:TMP + 1024], xt, xb(c + 2, 1024), ALU.subtract))
            p.emit("vector", lambda v: v.drain(fusable=False))
            tok_xn = p.emit("vector", lambda v, c=c: v.tensor_tensor(
                AB[:, LNS:LNS + 1024], XF[:, TMP:TMP + 1024],
                xb(c + 7, 1024), ALU.mult), inc=(sv, 1))
            p.wait("tensor", sv, tok_xn)
            p.bank_wait("tensor", "T")
            tok_tr = None
            for cc in range(ND):
                fn = lambda t, cc=cc: t.transpose(
                    psT[:, 128 * cc:128 * cc + 128],
                    AB[:, LNS + 128 * cc:LNS + 128 * cc + 128], ident[:])
                tok_tr = p.emit("tensor", fn,
                                inc=(st, 1) if cc == ND - 1 else None)
            p.wait("scalar", st, tok_tr)
            dst = AB[:, XN:XN + 16384].rearrange(
                "q (dc t) -> q dc t", dc=ND)[:, :, 128 * i:128 * i + 128]
            tok_ev = p.emit("scalar", lambda s, dst=dst: s.copy(
                dst, psT_v), inc=(ss, 1))
            p.bank_set("T", ss, tok_ev)
            p.bank_set("LNS", ss, tok_ev)
            xn_ready[i] = tok_ev
        return xn_ready

    def emit_wt_load(h, area_free_tok):
        p.wait("gpsimd", st, area_free_tok)
        toks = {}
        for wi, (wd, dsto) in enumerate(((wqd, WQT), (wkd, WKT),
                                         (wvd, WVT))):
            src = wd.ap().rearrange(
                "(dc q) f -> q dc f", q=128)[:, :, 512 * h:512 * h + 512]
            dst = AB[:, dsto:dsto + 4096].rearrange(
                "q (dc f) -> q dc f", dc=ND)
            toks[wi] = p.emit("gpsimd", lambda g, s=src, d=dst:
                              g.dma_start(d, s), inc=(swt[wi], 16))
        srco = wod.ap().rearrange("(hh fc q) d -> hh q fc d", q=128, fc=4)[h]
        dsto_ = AB[:, WOT:WOT + 4096].rearrange("q (fc d) -> q fc d", fc=4)
        toks[3] = p.emit("gpsimd", lambda g, s=srco, d=dsto_:
                         g.dma_start(d, s), inc=(swt[3], 16))
        return toks

    gate_state = {"c": [0, 0], "ex": [0, 0], "tr": [0, 0]}
    xfinal = [0] * NT

    def emit_batch(b, prev_ffn_end_tok, prev_store_toks, tok_bm, tok_gw):
        xload = [0] * NT
        for i in range(NT):
            if prev_store_toks is not None:
                p.wait("sync", so, prev_store_toks[i])
            xload[i] = p.emit("sync", lambda s, b=b, i=i: s.dma_start(
                XF[:, XR + 1024 * i:XR + 1024 * i + 1024], x_h[b, i]),
                inc=(sx, 16))

        if b == 0:
            tap("x0", XF[:, XR:XR + 8192], 8192, sx, xload[NT - 1], f32=True)
        xn_ready = emit_ln(b, xload, pre_sem=st, pre_tok=prev_ffn_end_tok)
        if b == 0:
            tap("xn", AB[:, XN:XN + 8192], 8192, ss, xn_ready[NT - 1])
            tap("lns", AB[:, LNS:LNS + 1024], 1024, sv, xn_ready[NT - 1])
            tap("stat", XF[:, STAT:STAT + 16], 16, sv, xn_ready[NT - 1],
                f32=True)

        attn_half_end = prev_ffn_end_tok or 0
        last_OL = 0
        attn_u_last = 0
        for h in range(2):
            tok_wt = emit_wt_load(h, attn_half_end)

            banks = [("Alo", psA, 0), ("Ahi", psA, 512),
                     ("Blo", psB, 0), ("Bhi", psB, 512)]
            kv_last = 0
            for jb in range(4):
                for fc in range(4):
                    key, ps, co = banks[(jb * 4 + fc) % 4]
                    p.bank_wait("tensor", key)
                    p.wait("tensor", swt[1], tok_wt[1])
                    p.wait("tensor", ss, xn_ready[4 * jb + 3])
                    tok_mm = None
                    for dc in range(ND):
                        fn = lambda t, ps=ps, co=co, dc=dc, fc=fc, jb=jb: \
                            t.matmul(ps[:, co:co + 512],
                                     AB[:, WKT + 512 * dc + 128 * fc:
                                        WKT + 512 * dc + 128 * fc + 128],
                                     AB[:, XN + 2048 * dc + 512 * jb:
                                        XN + 2048 * dc + 512 * jb + 512],
                                     start=(dc == 0), stop=(dc == ND - 1))
                        tok_mm = p.emit("tensor", fn,
                                        inc=(st, 1) if dc == ND - 1 else None)
                    p.wait("vector", st, tok_mm)
                    tok_ev = p.emit(
                        "vector", lambda v, ps=ps, co=co, fc=fc, jb=jb:
                        v.tensor_copy(
                            AB[:, KF + 2048 * fc + 512 * jb:
                               KF + 2048 * fc + 512 * jb + 512],
                            ps[:, co:co + 512]), inc=(sv, 1))
                    p.bank_set(key, sv, tok_ev)
                    kv_last = tok_ev
            if b == 0 and h == 0:
                tap("kf", AB[:, KF:KF + 8192], 8192, sv, kv_last)

            for i in range(NT):
                key, co = ("Clo", 0) if i % 2 == 0 else ("Chi", 512)
                p.bank_wait("tensor", key)
                p.wait("tensor", swt[2], tok_wt[2])
                p.wait("tensor", ss, xn_ready[i])
                tok_mm = None
                for dc in range(ND):
                    fn = lambda t, co=co, dc=dc, i=i: t.matmul(
                        psC[:, co:co + 512],
                        AB[:, XN + 2048 * dc + 128 * i:
                           XN + 2048 * dc + 128 * i + 128],
                        AB[:, WVT + 512 * dc:WVT + 512 * dc + 512],
                        start=(dc == 0), stop=(dc == ND - 1))
                    tok_mm = p.emit("tensor", fn,
                                    inc=(st, 1) if dc == ND - 1 else None)
                p.wait("vector", st, tok_mm)
                tok_ev = p.emit("vector", lambda v, co=co, i=i: v.tensor_copy(
                    AB[:, VF + 512 * i:VF + 512 * i + 512],
                    psC[:, co:co + 512]), inc=(sv, 1))
                p.bank_set(key, sv, tok_ev)
                kv_last = tok_ev
            if b == 0 and h == 0:
                tap("vf", AB[:, VF:VF + 8192], 8192, sv, kv_last)

            for ib in range(4):
                p.wait("tensor", sv, kv_last)
                p.wait("tensor", swt[0], tok_wt[0])
                p.wait("tensor", swt[3], tok_wt[3])
                tok_q = 0
                for fc in range(4):
                    p.bank_wait("tensor", "D")
                    tok_mm = None
                    for dc in range(ND):
                        fn = lambda t, fc=fc, dc=dc, ib=ib: t.matmul(
                            psD[:, 0:512],
                            AB[:, WQT + 512 * dc + 128 * fc:
                               WQT + 512 * dc + 128 * fc + 128],
                            AB[:, XN + 2048 * dc + 512 * ib:
                               XN + 2048 * dc + 512 * ib + 512],
                            start=(dc == 0), stop=(dc == ND - 1))
                        tok_mm = p.emit("tensor", fn,
                                        inc=(st, 1) if dc == ND - 1 else None)
                    p.wait("vector", st, tok_mm)
                    tok_q = p.emit("vector", lambda v, fc=fc: v.tensor_copy(
                        AB[:, QB + 512 * fc:QB + 512 * fc + 512],
                        psD[:, 0:512]), inc=(sv, 1))
                    p.bank_set("D", sv, tok_q)
                p.wait("tensor", sv, tok_q)
                if b == 0 and h == 0 and ib == 0:
                    tap("qb", AB[:, QB:QB + 2048], 2048, sv, tok_q)

                tok_oa_ev = 0
                for pr in range(4):
                    okey, oco = ("Clo", 0) if pr % 2 == 0 else ("Chi", 512)
                    tok_P = {}
                    tok_OL = {}
                    njc = 4 * ib + 4
                    for jc in range(njc):
                        u = jc - 4 * ib
                        off = 128 * u if u > 0 else 0
                        w = 512 - off
                        skey, sps = ("A", psA) if jc % 2 == 0 else ("B", psB)
                        p.bank_wait("tensor", skey)
                        sA = lambda t, sps=sps, jc=jc, pr=pr, off=off, w=w: \
                            t.matmul(sps[:, 0:w],
                                     AB[0:64, KF + 2048 * pr + 128 * jc:
                                        KF + 2048 * pr + 128 * jc + 128],
                                     AB[0:64, QB + 512 * pr + off:
                                        QB + 512 * pr + 512],
                                     start=True, stop=True,
                                     tile_position=(0, 0))
                        sB = lambda t, sps=sps, jc=jc, pr=pr, off=off, w=w: \
                            t.matmul(sps[:, 512:512 + w],
                                     AB[64:128, KF + 2048 * pr + 128 * jc:
                                        KF + 2048 * pr + 128 * jc + 128],
                                     AB[64:128, QB + 512 * pr + off:
                                        QB + 512 * pr + 512],
                                     start=True, stop=True,
                                     tile_position=(64, 0))
                        p.emit("tensor", sA)
                        tok_s = p.emit("tensor", sB, inc=(st, 1))
                        p.wait("scalar", st, tok_s)
                        tb = TT0 if jc % 2 == 0 else TT1
                        sin = sps.ap().rearrange(
                            "q (a t) -> q a t", a=2)[:, :, 0:w]
                        tout = XF[:, tb:tb + 1024].rearrange(
                            "q (a t) -> q a t", a=2)[:, :, 0:w]
                        tok_th = p.emit("scalar", lambda s, i_=sin, o=tout:
                                        s.activation(o, i_, AF.Tanh,
                                                     scale=SCALE / CLAMP),
                                        inc=(ss, 1))
                        p.bank_set(skey, ss, tok_th)
                        pg = jc % 4
                        p.wait("scalar", st, tok_OL.get(jc - 4, last_OL))
                        pin = XF[:, tb:tb + 1024].rearrange(
                            "q (a t) -> q a t", a=2)[:, :, 0:w]
                        pout = AB[:, PP + 1024 * pg:PP + 1024 * pg + 1024] \
                            .rearrange("q (a t) -> q a t", a=2)[:, :, 0:w]
                        tok_P[jc] = p.emit(
                            "scalar", lambda s, i_=pin, o=pout:
                            s.activation(o, i_, AF.Exp, scale=CLAMP),
                            inc=(ss, 1))
                        need_sem, need_tok = ss, tok_P[jc]
                        if u >= 0:
                            p.wait("vector", ss, tok_P[jc])
                            p.emit("vector", lambda v, pg=pg: v.tensor_mul(
                                AB[:, PP + 1024 * pg:PP + 1024 * pg + 128],
                                AB[:, PP + 1024 * pg:PP + 1024 * pg + 128],
                                tri[:]))
                            tok_m = p.emit("vector", lambda v, pg=pg:
                                           v.tensor_mul(
                                               AB[:, PP + 1024 * pg + 512:
                                                  PP + 1024 * pg + 640],
                                               AB[:, PP + 1024 * pg + 512:
                                                  PP + 1024 * pg + 640],
                                               tri[:]), inc=(sv, 1))
                            need_sem, need_tok = sv, tok_m
                        if b == 0 and h == 0 and ib == 0 and pr == 0 \
                                and jc == 0:
                            tap("pp", AB[:, PP:PP + 1024], 1024,
                                need_sem, need_tok)
                        p.wait("tensor", need_sem, need_tok)
                        if jc == 0:
                            p.bank_wait("tensor", okey)
                            p.bank_wait("tensor", "D")
                        first = jc == 0
                        hhA, hhB = 2 * pr, 2 * pr + 1
                        oA = lambda t, jc=jc, hh=hhA, pg=pg, off=off, w=w, \
                            oco=oco, first=first: t.matmul(
                            psC[0:64, oco + off:oco + off + w],
                            AB[:, VF + 512 * jc + 64 * hh:
                               VF + 512 * jc + 64 * hh + 64],
                            AB[:, PP + 1024 * pg:PP + 1024 * pg + w],
                            start=first, stop=True, tile_position=(0, 0),
                            skip_group_check=True)
                        oB = lambda t, jc=jc, hh=hhB, pg=pg, off=off, w=w, \
                            oco=oco, first=first: t.matmul(
                            psC[64:128, oco + off:oco + off + w],
                            AB[:, VF + 512 * jc + 64 * hh:
                               VF + 512 * jc + 64 * hh + 64],
                            AB[:, PP + 1024 * pg + 512:
                               PP + 1024 * pg + 512 + w],
                            start=first, stop=True, tile_position=(0, 64),
                            skip_group_check=True)
                        lA = lambda t, pg=pg, off=off, w=w, first=first: \
                            t.matmul(psD[0:64, off:off + w], ones64[:],
                                     AB[:, PP + 1024 * pg:PP + 1024 * pg + w],
                                     start=first, stop=True,
                                     tile_position=(0, 0),
                                     skip_group_check=True)
                        lB = lambda t, pg=pg, off=off, w=w, first=first: \
                            t.matmul(psD[64:128, off:off + w], ones64[:],
                                     AB[:, PP + 1024 * pg + 512:
                                        PP + 1024 * pg + 512 + w],
                                     start=first, stop=True,
                                     tile_position=(0, 64),
                                     skip_group_check=True)
                        p.emit("tensor", oA)
                        p.emit("tensor", oB)
                        p.emit("tensor", lA)
                        tok_OL[jc] = p.emit("tensor", lB, inc=(st, 1))
                    last_OL = tok_OL[njc - 1]
                    p.wait("vector", st, last_OL)
                    p.emit("vector", lambda v: v.reciprocal(
                        XF[:, REC:REC + 512], psD[:, 0:512]))
                    p.emit("vector", lambda v: v.drain(fusable=False))
                    p.emit("vector", lambda v, oco=oco, pr=pr: v.tensor_mul(
                        AB[0:64, OA + 512 * pr:OA + 512 * pr + 512],
                        psC[0:64, oco:oco + 512],
                        XF[0:64, REC:REC + 512]))
                    tok_oa = p.emit("vector", lambda v, oco=oco, pr=pr:
                                    v.tensor_mul(
                                        AB[64:128, OA + 512 * pr:
                                           OA + 512 * pr + 512],
                                        psC[64:128, oco:oco + 512],
                                        XF[64:128, REC:REC + 512]),
                                    inc=(sv, 1))
                    p.bank_set(okey, sv, tok_oa)
                    p.bank_set("D", sv, tok_oa)
                    tok_oa_ev = tok_oa
                    if b == 0 and h == 0 and ib == 0 and pr == 0:
                        tap("oa", AB[:, OA:OA + 512], 512, sv, tok_oa)

                p.wait("tensor", sv, tok_oa_ev)
                ukeys = [("D", psD, 0), ("Alo", psA, 0)]
                for ii in range(4):
                    for dd in range(2):
                        key, ps, co = ukeys[(ii * 2 + dd) % 2]
                        p.bank_wait("tensor", key)
                        tok_mm = None
                        for fc in range(4):
                            fn = lambda t, ps=ps, co=co, fc=fc, ii=ii, dd=dd: \
                                t.matmul(
                                    ps[:, co:co + 512],
                                    AB[:, OA + 512 * fc + 128 * ii:
                                       OA + 512 * fc + 128 * ii + 128],
                                    AB[:, WOT + 1024 * fc + 512 * dd:
                                       WOT + 1024 * fc + 512 * dd + 512],
                                    start=(fc == 0), stop=(fc == 3))
                            tok_mm = p.emit("tensor", fn,
                                            inc=(st, 1) if fc == 3 else None)
                        p.wait("vector", st, tok_mm)
                        xs = XF[:, XR + 1024 * (4 * ib + ii) + 512 * dd:
                                XR + 1024 * (4 * ib + ii) + 512 * dd + 512]
                        tok_u = p.emit("vector", lambda v, xs=xs, ps=ps,
                                       co=co: v.tensor_add(
                                           xs, xs, ps[:, co:co + 512]),
                                       inc=(sv, 1))
                        p.bank_set(key, sv, tok_u)
                        attn_u_last = tok_u
            attn_half_end = st.n

        attn_end_st = st.n
        attn_end_sv = attn_u_last
        if b == 0:
            tap("xattn", XF[:, XR:XR + 8192], 8192, sv, attn_end_sv,
                f32=True)

        xn2_ready = emit_ln(b, None, pre_sem=sv, pre_tok=attn_end_sv)
        if b == 0:
            tap("xn2", AB[:, XN:XN + 8192], 8192, ss, xn2_ready[NT - 1])

        gate_done = [0] * NT
        p.bank_wait("tensor", "D")
        p.wait("tensor", sc, tok_gw)
        tok_gmm = 0
        for i in range(NT):
            p.wait("tensor", ss, xn2_ready[i])
            for dc in range(ND):
                fn = lambda t, dc=dc, i=i: t.matmul(
                    psD[:, 8 * i:8 * i + 8],
                    AB[:, XN + 2048 * dc + 128 * i:
                       XN + 2048 * dc + 128 * i + 128],
                    gwS[:, 8 * dc:8 * dc + 8],
                    start=(dc == 0), stop=(dc == ND - 1))
                tok_gmm = p.emit("tensor", fn,
                                 inc=(st, 1) if dc == ND - 1 else None)
        p.wait("vector", st, tok_gmm)
        gbbb = bass.AP(XF.ap().tensor, GBB, [[XF_COLS, 128], [0, NT], [1, 8]])
        pdv = psD[:, 0:128].rearrange("q (i e) -> q i e", e=8)
        glv = XF[:, GL:GL + 128].rearrange("q (i e) -> q i e", e=8)
        tok_lg = p.emit("vector", lambda v: v.tensor_tensor(
            glv, pdv, gbbb, ALU.add), inc=(sv, 1))
        p.bank_set("D", sv, tok_lg)
        p.emit("vector", lambda v: v.drain(fusable=False))
        tok_nm = p.emit("vector", lambda v: v.reduce_max(
            XF[:, SGL:SGL + 16], glv, AX.X, negate=True), inc=(sv, 1))
        p.wait("scalar", sv, tok_nm)
        tok_ex = 0
        for i in range(NT):
            tok_ex = p.emit("scalar", lambda s, i=i: s.activation(
                XF[:, PEX + 8 * i:PEX + 8 * i + 8],
                XF[:, GL + 8 * i:GL + 8 * i + 8], AF.Exp,
                bias=XF[:, SGL + i:SGL + i + 1],
                accum_out=XF[:, SGL + 16 + i:SGL + 17 + i]),
                inc=(ss, 1) if i == NT - 1 else None)
        p.wait("vector", ss, tok_ex)

        def gview(base, g):
            return XF[:, base + 64 * g:base + 64 * g + 64].rearrange(
                "q (i e) -> q i e", e=8)

        def sview(base, g):
            return XF[:, base + 8 * g:base + 8 * g + 8]

        def sbc(base, g):
            return bass.AP(XF.ap().tensor, base + 8 * g,
                           [[XF_COLS, 128], [1, 8], [0, 8]])

        for g in range(2):
            p.emit("vector", lambda v, g=g: v.reduce_max(
                sview(SGL + 48, g), gview(PEX, g), AX.X))
        p.emit("vector", lambda v: v.reciprocal(
            XF[:, SGL + 32:SGL + 48], XF[:, SGL + 16:SGL + 32]))
        for g in range(2):
            p.emit("vector", lambda v, g=g: v.tensor_tensor(
                gview(M1, g), gview(PEX, g), sbc(SGL + 48, g), ALU.is_ge))
        for g in range(2):
            p.emit("vector", lambda v, g=g: v.scalar_tensor_tensor(
                gview(T2, g), gview(M1, g), -BIG, gview(PEX, g),
                ALU.mult, ALU.add))
        for g in range(2):
            p.emit("vector", lambda v, g=g: v.reduce_max(
                sview(SGL + 64, g), gview(T2, g), AX.X))
        for g in range(2):
            p.emit("vector", lambda v, g=g: v.tensor_tensor(
                gview(T2, g), gview(T2, g), sbc(SGL + 64, g), ALU.is_ge))
        for g in range(2):
            p.emit("vector", lambda v, g=g: v.tensor_tensor(
                gview(M1, g), gview(M1, g), gview(T2, g), ALU.add))
        for g in range(2):
            p.emit("vector", lambda v, g=g: v.tensor_tensor(
                gview(M1, g), gview(M1, g), gview(PEX, g), ALU.mult))
        for g in range(2):
            p.emit("vector", lambda v, g=g: v.tensor_tensor(
                gview(CAL, g), gview(M1, g), sbc(SGL + 32, g), ALU.mult))
        p.emit("vector", lambda v: v.drain(fusable=False))
        tok_cb = p.emit("vector", lambda v: v.tensor_copy(
            cb8[:], XF[:, CAL:CAL + 128]), inc=(sv, 1))
        if b == 0:
            tap("sgl", XF[:, SGL:SGL + 80], 80, sv, tok_cb, f32=True)
            tap("m1", XF[:, M1:M1 + 128], 128, sv, tok_cb, f32=True)
        p.wait("tensor", sv, tok_cb)
        tok_gd = 0
        for g in range(2):
            p.bank_wait("tensor", "T")
            tok_tr = None
            for t_ in range(8):
                i = 8 * g + t_
                fn = lambda t, i=i, t_=t_: t.transpose(
                    psT[0:8, 128 * t_:128 * t_ + 128],
                    cb8[:, 8 * i:8 * i + 8], ident[:])
                tok_tr = p.emit("tensor", fn,
                                inc=(st, 1) if t_ == 7 else None)
            p.wait("vector", st, tok_tr)
            tok_gd = p.emit("vector", lambda v, g=g: v.tensor_copy(
                bct[0:8, 1024 * g:1024 * g + 1024], psT[0:8, 0:1024]),
                inc=(sv, 1))
            p.bank_set("T", sv, tok_gd)
        for i in range(NT):
            gate_done[i] = tok_gd

        hkeys = [("Alo", psA, 0), ("Ahi", psA, 512), ("Blo", psB, 0)]
        obanks = [[("Clo", psC, 0), ("Chi", psC, 512)],
                  [("D", psD, 0), ("Bhi", psB, 512)]]
        tok_wload = {}
        tok_expert_done = {}
        tok_g_read = {}

        def emit_wdma(k):
            j = k % 2
            p.wait("gpsimd", st, tok_expert_done.get(k - 2, attn_end_st))
            if k == 0:
                srcs = [(sw1d.ap(), W1BUF[j]), (sw2d.ap(), W2BUF[j])]
            else:
                srcs = [(w1d.ap()[k - 1], W1BUF[j]),
                        (w2d.ap()[k - 1], W2BUF[j])]
            tk = 0
            for src, dsto in srcs:
                s2 = src.rearrange("(c q) f -> q c f", q=128)
                d2 = AB[:, dsto:dsto + 8192].rearrange(
                    "q (c f) -> q c f", c=8)
                tk = p.emit("gpsimd", lambda g, s=s2, d=d2: g.dma_start(
                    d, s), inc=(swe[j], 16))
            tok_wload[k] = tk

        emit_wdma(0)
        emit_wdma(1)
        if b == 0:
            tap("w1c", AB[:, KF:KF + 8192], 8192, swe[0], tok_wload[0])
        for k in range(9):
            j = k % 2
            p.wait("tensor", swe[j], tok_wload[k])
            for blk in range(4):
                gp = blk % 2
                gbase = GBUF[gp]
                p.wait("tensor", ss, xn2_ready[4 * blk + 3])
                gate_tok = tok_g_read.get(gp)
                tok_gl = None
                for ffc in range(ND):
                    key, ps, co = hkeys[ffc % 3]
                    p.bank_wait("tensor", key)
                    tok_mm = None
                    for dc in range(ND):
                        fn = lambda t, ps=ps, co=co, dc=dc, ffc=ffc, blk=blk, \
                            j=j: t.matmul(
                            ps[:, co:co + 512],
                            AB[:, W1BUF[j] + 1024 * dc + 128 * ffc:
                               W1BUF[j] + 1024 * dc + 128 * ffc + 128],
                            AB[:, XN + 2048 * dc + 512 * blk:
                               XN + 2048 * dc + 512 * blk + 512],
                            start=(dc == 0), stop=(dc == ND - 1))
                        tok_mm = p.emit("tensor", fn,
                                        inc=(st, 1) if dc == ND - 1 else None)
                    if (DBG == "h0" and b == 0 and k == 0 and blk == 0
                            and ffc == 0):
                        p.wait("vector", st, tok_mm)
                        tok_h0 = p.emit("vector", lambda v, ps=ps, co=co:
                                        v.tensor_copy(
                                            XF[:, TMP:TMP + 512],
                                            ps[:, co:co + 512]), inc=(sv, 1))
                        tap("h0", XF[:, TMP:TMP + 512], 512, sv, tok_h0,
                            f32=True)
                    p.wait("scalar", st, tok_mm)
                    if gate_tok is not None:
                        p.wait("scalar", st, gate_tok)
                    if k == 0:
                        bias = sb1f[:, ffc:ffc + 1]
                    else:
                        bias = b1f[:, 8 * (k - 1) + ffc:8 * (k - 1) + ffc + 1]
                    p.wait("scalar", sc, tok_bm)
                    tok_gl = p.emit("scalar", lambda s, gb=gbase, ffc=ffc,
                                    ps=ps, co=co, bias=bias: s.activation(
                                        AB[:, gb + 512 * ffc:
                                           gb + 512 * ffc + 512],
                                        ps[:, co:co + 512], AF.Gelu,
                                        bias=bias), inc=(ss, 1))
                    p.bank_set(key, ss, tok_gl)
                p.wait("tensor", ss, tok_gl)
                if b == 0 and k == 0 and blk == 0:
                    tap("g", AB[:, GBUF[0]:GBUF[0] + 4096], 4096, ss, tok_gl)
                for ii in range(4):
                    i_t = 4 * blk + ii
                    obk = obanks[ii % 2]
                    for okey, _, _ in obk:
                        p.bank_wait("tensor", okey)
                    tok_out = None
                    for ffc in range(ND):
                        for dd in range(2):
                            okey, ops_, oco = obk[dd]
                            last = (k != 0 and ffc == ND - 1 and dd == 1)
                            fn = lambda t, ops_=ops_, oco=oco, ffc=ffc, \
                                ii=ii, dd=dd, gb=gbase, j=j, last=last: \
                                t.matmul(
                                ops_[:, oco:oco + 512],
                                AB[:, gb + 512 * ffc + 128 * ii:
                                   gb + 512 * ffc + 128 * ii + 128],
                                AB[:, W2BUF[j] + 1024 * ffc + 512 * dd:
                                   W2BUF[j] + 1024 * ffc + 512 * dd + 512],
                                start=(ffc == 0), stop=last)
                            tok_out = p.emit("tensor", fn,
                                             inc=(st, 1) if last else None)
                    if k == 0:
                        p.wait("tensor", sv, gate_done[i_t])
                        p.wait("tensor", sc, tok_bm)
                        for dd in range(2):
                            okey, ops_, oco = obk[dd]
                            fnb = lambda t, ops_=ops_, oco=oco, i_t=i_t, \
                                dd=dd: t.matmul(
                                ops_[:, oco:oco + 512],
                                bct[:, 128 * i_t:128 * i_t + 128],
                                bmat[:, 512 * dd:512 * dd + 512],
                                start=False, stop=False)
                            p.emit("tensor", fnb)
                            fns = lambda t, ops_=ops_, oco=oco, dd=dd, \
                                last=(dd == 1): t.matmul(
                                ops_[:, oco:oco + 512],
                                ones1[:, 0:128],
                                bmatS[:, 512 * dd:512 * dd + 512],
                                start=False, stop=last)
                            tok_out = p.emit("tensor", fns,
                                             inc=(st, 1) if dd == 1 else None)
                    p.wait("vector", st, tok_out)
                    for dd in range(2):
                        okey, ops_, oco = obk[dd]
                        xs = XF[:, XR + 1024 * i_t + 512 * dd:
                                XR + 1024 * i_t + 512 * dd + 512]
                        if k == 0:
                            tok_e = p.emit(
                                "vector", lambda v, xs=xs, ops_=ops_,
                                oco=oco: v.tensor_add(
                                    xs, xs, ops_[:, oco:oco + 512]),
                                inc=(sv, 1))
                        else:
                            cs = XF[:, CAL + 8 * i_t + (k - 1):
                                    CAL + 8 * i_t + k]
                            tok_e = p.emit(
                                "vector", lambda v, xs=xs, ops_=ops_,
                                oco=oco, cs=cs: v.scalar_tensor_tensor(
                                    xs, ops_[:, oco:oco + 512], cs, xs,
                                    ALU.mult, ALU.add), inc=(sv, 1))
                        p.bank_set(okey, sv, tok_e)
                        if k == 8:
                            xfinal[i_t] = tok_e
                tok_g_read[gp] = st.n
            tok_expert_done[k] = st.n
            if k + 2 <= 8:
                emit_wdma(k + 2)
            if b == 0 and k == 0:
                tap("xs0", XF[:, XR:XR + 8192], 8192, sv, sv.n, f32=True)

        store_toks = [0] * NT
        for i in range(NT):
            p.wait("sync", sv, xfinal[i])
            store_toks[i] = p.emit("sync", lambda s, b=b, i=i: s.dma_start(
                out_h[b, i], XF[:, XR + 1024 * i:XR + 1024 * i + 1024]),
                inc=(so, 16))
        return tok_expert_done[8], store_toks

    tok_bm, tok_gw = emit_constants()
    prev_end, prev_stores = None, None
    for b in range(NB):
        prev_end, prev_stores = emit_batch(b, prev_end, prev_stores,
                                           tok_bm, tok_gw)

    p.wait("sync", so, so.n)

    with nc.Block() as blk:
        @blk.sync
        def _(e):
            for fn in p.st["sync"]:
                fn(e)

        @blk.tensor
        def _(e):
            for fn in p.st["tensor"]:
                fn(e)

        @blk.vector
        def _(e):
            for fn in p.st["vector"]:
                fn(e)

        @blk.scalar
        def _(e):
            for fn in p.st["scalar"]:
                fn(e)

        @blk.gpsimd
        def _(e):
            for fn in p.st["gpsimd"]:
                fn(e)

    return nc


def kernel(**inputs):
    x = np.ascontiguousarray(inputs["L0_attn_x"])
    nc = build_program()
    b1 = np.asarray(inputs["L0_ffn_b1"])
    sb1 = np.asarray(inputs["L0_ffn_sb1"])
    gw = np.asarray(inputs["L0_ffn_gate_w"])
    base = {
        "wqt": np.ascontiguousarray(np.asarray(inputs["L0_attn_Wq"]).T),
        "wkt": np.ascontiguousarray(np.asarray(inputs["L0_attn_Wk"]).T),
        "wvt": np.ascontiguousarray(np.asarray(inputs["L0_attn_Wv"]).T),
        "wot": np.ascontiguousarray(np.asarray(inputs["L0_attn_Wo"]).T),
        "gwl": np.ascontiguousarray(
            gw.reshape(8, 128, 8).transpose(1, 0, 2).reshape(128, 64)),
        "gbv": np.ascontiguousarray(inputs["L0_ffn_gate_b"]),
        "sw1": np.ascontiguousarray(inputs["L0_ffn_sw1"]),
        "sb1l": np.ascontiguousarray(sb1.reshape(8, 128).T),
        "sw2": np.ascontiguousarray(inputs["L0_ffn_sw2"]),
        "sb2": np.ascontiguousarray(inputs["L0_ffn_sb2"]),
        "w1": np.ascontiguousarray(inputs["L0_ffn_w1"]),
        "b1l": np.ascontiguousarray(
            b1.reshape(8, 8, 128).transpose(2, 0, 1).reshape(128, 64)),
        "w2": np.ascontiguousarray(inputs["L0_ffn_w2"]),
        "b2": np.ascontiguousarray(inputs["L0_ffn_b2"]),
    }
    in_maps = []
    for c in range(8):
        m = dict(base)
        m["x"] = np.ascontiguousarray(x[NB * c:NB * c + NB])
        in_maps.append(m)
    res = run_bass_kernel_spmd(nc, in_maps, core_ids=list(range(8)))
    if os.environ.get("KDBG", ""):
        kernel.dbg = (res.results[0].get("dbgb"), res.results[0].get("dbgf"))
    return np.concatenate([r["out"] for r in res.results], axis=0)


if __name__ == "__main__":
    import time
    t0 = time.time()
    nc = build_program()
    print("built ok in %.1fs" % (time.time() - t0))


# version 06
# speedup vs baseline: 1.0585x; 1.0585x over previous
import os
import numpy as np
import concourse.bass as bass
import concourse.mybir as mybir
from concourse.bass_utils import run_bass_kernel_spmd

F32 = mybir.dt.float32
BF16 = mybir.dt.bfloat16
AF = mybir.ActivationFunctionType
ALU = mybir.AluOpType
AX = mybir.AxisListType

NB = 2
N = 2048
D = 1024
NT = N // 128
ND = D // 128
DH = 64
SCALE = DH ** -0.5
CLAMP = 50.0
NEXP = 8
BIG = 1e30
EPS = 1e-5

XN = 0
KF = 16384
VF = 24576
WQT = 32768
WKT = 36864
WVT = 40960
WOT = 45056
QB = 49152
OA = 51200
PP = 53248
GX = 55296
LNS = 57344
AB_COLS = 58368
W1BUF = [KF, WQT]
W2BUF = [VF, WVT]
GBUF = [QB, PP]

XR = 0
TT0 = 16384
TT1 = 17408
SQ = 18432
REC = 19456
CAL = 19968
STAT = 20096
GL = 20224
PEX = 20352
M1 = 20480
T2 = 20608
SGL = 20736
GBB = 20816
GBS = 20824
TMP = SQ
XF_COLS = 20832

OVL = {
    "A": ("A", "Alo", "Ahi"), "Alo": ("A", "Alo"), "Ahi": ("A", "Ahi"),
    "B": ("B", "Blo", "Bhi"), "Blo": ("B", "Blo"), "Bhi": ("B", "Bhi"),
    "Clo": ("Clo",), "Chi": ("Chi",), "D": ("D",), "T": ("T",),
    "LNS": ("LNS",),
}


class Sem:
    __slots__ = ("h", "n", "waited")

    def __init__(self, nc, name):
        self.h = nc.alloc_semaphore(name)
        self.n = 0
        self.waited = {}


class Prog:
    def __init__(self, nc):
        self.nc = nc
        self.st = {k: [] for k in ("sync", "tensor", "vector", "scalar",
                                   "gpsimd")}
        self.free = {}

    def emit(self, eng, fn, inc=None):
        if inc is None:
            self.st[eng].append(fn)
            return None
        sem, amt = inc
        sem.n += amt

        def play(e, fn=fn, h=sem.h, amt=amt):
            fn(e).then_inc(h, amt)

        self.st[eng].append(play)
        return sem.n

    def wait(self, eng, sem, tok):
        if tok is None or tok <= 0:
            return
        if sem.waited.get(eng, 0) >= tok:
            return
        sem.waited[eng] = tok
        self.st[eng].append(lambda e, h=sem.h, t=tok: e.wait_ge(h, t))

    def bank_wait(self, eng, key):
        for k in OVL[key]:
            if k in self.free:
                sem, tok = self.free[k]
                self.wait(eng, sem, tok)

    def bank_set(self, key, sem, tok):
        self.free[key] = (sem, tok)


def build_program():
    nc = bass.Bass("TRN2", target_bir_lowering=False)

    xd = nc.dram_tensor("x", [NB, N, D], F32, kind="ExternalInput")
    wqd = nc.dram_tensor("wqt", [D, D], F32, kind="ExternalInput")
    wkd = nc.dram_tensor("wkt", [D, D], F32, kind="ExternalInput")
    wvd = nc.dram_tensor("wvt", [D, D], F32, kind="ExternalInput")
    wod = nc.dram_tensor("wot", [D, D], F32, kind="ExternalInput")
    gwd = nc.dram_tensor("gwl", [128, 64], F32, kind="ExternalInput")
    gbd = nc.dram_tensor("gbv", [NEXP], F32, kind="ExternalInput")
    sw1d = nc.dram_tensor("sw1", [D, D], F32, kind="ExternalInput")
    sb1d = nc.dram_tensor("sb1l", [128, 8], F32, kind="ExternalInput")
    sw2d = nc.dram_tensor("sw2", [D, D], F32, kind="ExternalInput")
    sb2d = nc.dram_tensor("sb2", [D], F32, kind="ExternalInput")
    w1d = nc.dram_tensor("w1", [NEXP, D, D], F32, kind="ExternalInput")
    b1d = nc.dram_tensor("b1l", [128, 64], F32, kind="ExternalInput")
    w2d = nc.dram_tensor("w2", [NEXP, D, D], F32, kind="ExternalInput")
    b2d = nc.dram_tensor("b2", [NEXP, D], F32, kind="ExternalInput")
    outd = nc.dram_tensor("out", [NB, N, D], F32, kind="ExternalOutput")

    AB = nc.alloc_sbuf_tensor("AB", [128, AB_COLS], BF16)
    XF = nc.alloc_sbuf_tensor("XF", [128, XF_COLS], F32)
    ident = nc.alloc_sbuf_tensor("ident", [128, 128], BF16)
    tri = nc.alloc_sbuf_tensor("tri", [128, 128], BF16)
    ones64 = nc.alloc_sbuf_tensor("ones64", [128, 64], BF16)
    ones1 = nc.alloc_sbuf_tensor("ones1", [1, 128], BF16)
    onesP = nc.alloc_sbuf_tensor("onesP", [128, 128], BF16)
    gbhl = nc.alloc_sbuf_tensor("gbhl", [1, 16], BF16)
    gwS = nc.alloc_sbuf_tensor("gwS", [128, 64], BF16)
    b1f = nc.alloc_sbuf_tensor("b1f", [128, 64], F32)
    sb1f = nc.alloc_sbuf_tensor("sb1f", [128, 8], F32)
    bmat = nc.alloc_sbuf_tensor("bmat", [8, D], BF16)
    bmatS = nc.alloc_sbuf_tensor("bmatS", [1, D], BF16)
    bct = nc.alloc_sbuf_tensor("bct", [8, 2048], BF16)
    cb8 = nc.alloc_sbuf_tensor("cb8", [128, 128], BF16)
    cst = nc.alloc_sbuf_tensor("cst", [128, 4], F32)
    nc.const_aps.register(F32, EPS, cst[:, 0:1])
    nc.const_aps.register(F32, SCALE / CLAMP, cst[:, 1:2])
    nc.const_aps.register(F32, CLAMP, cst[:, 2:3])

    psA = nc.alloc_psum_tensor("psA", [128, 1024], F32)
    psB = nc.alloc_psum_tensor("psB", [128, 1024], F32)
    psC = nc.alloc_psum_tensor("psC", [128, 1024], F32)
    psD = nc.alloc_psum_tensor("psD", [128, 512], F32)
    psT = nc.alloc_psum_tensor("psT", [128, 1024], BF16)

    DBG = os.environ.get("KDBG", "")
    if DBG:
        dbgb = nc.dram_tensor("dbgb", [128, 8192], BF16,
                              kind="ExternalOutput")
        dbgf = nc.dram_tensor("dbgf", [128, 8192], F32,
                              kind="ExternalOutput")

    p = Prog(nc)
    sx = Sem(nc, "sx")
    sc = Sem(nc, "sc")
    swt = [Sem(nc, "swt%d" % i) for i in range(4)]
    swe = [Sem(nc, "swe0"), Sem(nc, "swe1")]
    sw = Sem(nc, "sw")
    so = Sem(nc, "so")
    st = Sem(nc, "st")
    sv = Sem(nc, "sv")
    ss = Sem(nc, "ss")
    sg = Sem(nc, "sg")

    def xb(col, n):
        return bass.AP(XF.ap().tensor, col, [[XF_COLS, 128], [0, n]])

    x_h = xd.ap().rearrange("b (i q) d -> b i q d", q=128)
    out_h = outd.ap().rearrange("b (i q) d -> b i q d", q=128)
    psT_v = psT.ap().rearrange("q (a t) -> q a t", a=ND)

    tapped = [False]

    def tap(name, src_ap, ncols, sem, tok, f32=False):
        if DBG != name or tapped[0]:
            return
        tapped[0] = True
        p.wait("sync", sem, tok)
        dst = (dbgf if f32 else dbgb).ap()[:, 0:ncols]
        tok_d = p.emit("sync", lambda s: s.dma_start(dst, src_ap),
                       inc=(so, 16))
        for eng in ("vector", "scalar", "tensor", "gpsimd"):
            p.wait(eng, so, tok_d)

    def emit_constants():
        p.emit("gpsimd", lambda g: g.memset(onesP[:], 1.0))
        p.emit("gpsimd", lambda g: g.memset(ones64[:], 1.0))
        p.emit("gpsimd", lambda g: g.memset(ones1[:], 1.0))
        p.emit("gpsimd", lambda g: g.affine_select(
            ident[:], onesP[:], [[-1, 128]], ALU.is_equal, 0.0,
            base=0, channel_multiplier=1))
        p.emit("gpsimd", lambda g: g.affine_select(
            tri[:], onesP[:], [[1, 128]], ALU.is_ge, 0.0,
            base=0, channel_multiplier=-1))
        p.emit("gpsimd", lambda g: g.memset(cst[:, 0:1], EPS))
        p.emit("gpsimd", lambda g: g.memset(cst[:, 1:2], SCALE / CLAMP))
        p.emit("gpsimd", lambda g: g.memset(cst[:, 2:3], CLAMP))
        tok_g = p.emit("gpsimd", lambda g: g.memset(cst[:, 3:4], 0.0),
                       inc=(sg, 1))
        p.wait("vector", sg, tok_g)
        sgtok[0] = tok_g
        p.emit("gpsimd", lambda g: g.dma_start(
            gwS[:], gwd.ap()), inc=(sc, 16))
        p.emit("gpsimd", lambda g: g.dma_start(bmat[:], b2d.ap()),
               inc=(sc, 16))
        p.emit("gpsimd", lambda g: g.dma_start(
            bmatS[:], sb2d.ap().rearrange("(a d) -> a d", a=1)),
            inc=(sc, 16))
        p.emit("sync", lambda s: s.dma_start(b1f[:], b1d.ap()),
               inc=(sc, 16))
        p.emit("sync", lambda s: s.dma_start(sb1f[:], sb1d.ap()),
               inc=(sc, 16))
        tok_gb = p.emit("sync", lambda s: s.dma_start(
            XF[0:1, GBS:GBS + 8],
            gbd.ap().rearrange("(a e) -> a e", a=1)), inc=(sc, 16))
        tok_bm = tok_gw = tok_gb
        p.wait("vector", sc, tok_gb)
        p.emit("vector", lambda v: v.tensor_copy(gbhl[0:1, 0:8],
                                                 XF[0:1, GBS:GBS + 8]))
        p.emit("vector", lambda v: v.drain(fusable=False))
        p.emit("vector", lambda v: v.scalar_tensor_tensor(
            XF[0:1, TMP:TMP + 8], gbhl[0:1, 0:8], -1.0,
            XF[0:1, GBS:GBS + 8], ALU.mult, ALU.add))
        p.emit("vector", lambda v: v.drain(fusable=False))
        tok_v = p.emit("vector", lambda v: v.tensor_copy(
            gbhl[0:1, 8:16], XF[0:1, TMP:TMP + 8]), inc=(sv, 1))
        p.wait("tensor", sv, tok_v)
        p.wait("tensor", sg, tok_g)
        p.emit("tensor", lambda t: t.matmul(
            psD[:, 0:8], ones1[:], gbhl[0:1, 0:8], start=True, stop=False))
        tok_t = p.emit("tensor", lambda t: t.matmul(
            psD[:, 0:8], ones1[:], gbhl[0:1, 8:16], start=False, stop=True),
            inc=(st, 1))
        p.wait("vector", st, tok_t)
        tok_v2 = p.emit("vector", lambda v: v.tensor_copy(
            XF[:, GBB:GBB + 8], psD[:, 0:8]), inc=(sv, 1))
        p.bank_set("D", sv, tok_v2)
        return tok_bm, tok_gw

    sgtok = [0]

    def emit_ln(b, xload_toks, pre_sem=None, pre_tok=None):
        xn_ready = [0] * NT
        tok_sq = 0
        for i in range(NT):
            if i == 0:
                if xload_toks is not None:
                    p.wait("vector", sx, xload_toks[NT - 1])
                    p.wait("scalar", sx, xload_toks[NT - 1])
                if pre_tok:
                    p.wait("vector", pre_sem, pre_tok)
                    p.wait("scalar", pre_sem, pre_tok)
            c = STAT + 8 * i
            xt = XF[:, XR + 1024 * i:XR + 1024 * i + 1024]
            p.emit("vector", lambda v, xt=xt, c=c: v.reduce_sum(
                XF[:, c:c + 1], xt, AX.X))
            tok_sq = p.emit("scalar", lambda s, xt=xt, c=c: s.activation(
                XF[:, SQ:SQ + 1024], xt, AF.Square,
                accum_out=XF[:, c + 1:c + 2]), inc=(ss, 1))
        p.wait("vector", ss, tok_sq)
        for i in range(NT):
            c = STAT + 8 * i
            p.emit("vector", lambda v, c=c: v.tensor_scalar_mul(
                XF[:, c + 3:c + 4], XF[:, c + 1:c + 2], 1.0 / D))
        for i in range(NT):
            c = STAT + 8 * i
            p.emit("vector", lambda v, c=c: v.tensor_scalar_mul(
                XF[:, c + 2:c + 3], XF[:, c:c + 1], 1.0 / D))
        for i in range(NT):
            c = STAT + 8 * i
            p.emit("vector", lambda v, c=c: v.tensor_mul(
                XF[:, c + 4:c + 5], XF[:, c + 2:c + 3], XF[:, c + 2:c + 3]))
        tok_var = 0
        for i in range(NT):
            c = STAT + 8 * i
            tok_var = p.emit("vector", lambda v, c=c: v.scalar_tensor_tensor(
                XF[:, c + 5:c + 6], XF[:, c + 4:c + 5], -1.0,
                XF[:, c + 3:c + 4], ALU.mult, ALU.add),
                inc=(sv, 1) if i == NT - 1 else None)
        p.wait("scalar", sv, tok_var)
        p.wait("scalar", sg, sgtok[0])
        tok_sd = 0
        for i in range(NT):
            c = STAT + 8 * i
            tok_sd = p.emit("scalar", lambda s, c=c: s.activation(
                XF[:, c + 6:c + 7], XF[:, c + 5:c + 6], AF.Sqrt, bias=EPS),
                inc=(ss, 1) if i == NT - 1 else None)
        p.wait("vector", ss, tok_sd)
        for i in range(NT):
            c = STAT + 8 * i
            p.emit("vector", lambda v, c=c: v.reciprocal(
                XF[:, c + 7:c + 8], XF[:, c + 6:c + 7]))
        for i in range(NT):
            c = STAT + 8 * i
            xt = XF[:, XR + 1024 * i:XR + 1024 * i + 1024]
            p.bank_wait("vector", "LNS")
            p.emit("vector", lambda v, xt=xt, c=c: v.tensor_tensor(
                XF[:, TMP:TMP + 1024], xt, xb(c + 2, 1024), ALU.subtract))
            p.emit("vector", lambda v: v.drain(fusable=False))
            tok_xn = p.emit("vector", lambda v, c=c: v.tensor_tensor(
                AB[:, LNS:LNS + 1024], XF[:, TMP:TMP + 1024],
                xb(c + 7, 1024), ALU.mult), inc=(sv, 1))
            p.wait("tensor", sv, tok_xn)
            p.bank_wait("tensor", "T")
            tok_tr = None
            for cc in range(ND):
                fn = lambda t, cc=cc: t.transpose(
                    psT[:, 128 * cc:128 * cc + 128],
                    AB[:, LNS + 128 * cc:LNS + 128 * cc + 128], ident[:])
                tok_tr = p.emit("tensor", fn,
                                inc=(st, 1) if cc == ND - 1 else None)
            p.wait("scalar", st, tok_tr)
            dst = AB[:, XN:XN + 16384].rearrange(
                "q (dc t) -> q dc t", dc=ND)[:, :, 128 * i:128 * i + 128]
            tok_ev = p.emit("scalar", lambda s, dst=dst: s.copy(
                dst, psT_v), inc=(ss, 1))
            p.bank_set("T", ss, tok_ev)
            p.bank_set("LNS", ss, tok_ev)
            xn_ready[i] = tok_ev
        return xn_ready

    def emit_wt_load(h, area_free_tok):
        p.wait("gpsimd", st, area_free_tok)
        toks = {}
        for wi, (wd, dsto) in enumerate(((wqd, WQT), (wkd, WKT),
                                         (wvd, WVT))):
            src = wd.ap().rearrange(
                "(dc q) f -> q dc f", q=128)[:, :, 512 * h:512 * h + 512]
            dst = AB[:, dsto:dsto + 4096].rearrange(
                "q (dc f) -> q dc f", dc=ND)
            toks[wi] = p.emit("gpsimd", lambda g, s=src, d=dst:
                              g.dma_start(d, s), inc=(swt[wi], 16))
        srco = wod.ap().rearrange("(hh fc q) d -> hh q fc d", q=128, fc=4)[h]
        dsto_ = AB[:, WOT:WOT + 4096].rearrange("q (fc d) -> q fc d", fc=4)
        toks[3] = p.emit("gpsimd", lambda g, s=srco, d=dsto_:
                         g.dma_start(d, s), inc=(swt[3], 16))
        return toks

    gate_state = {"c": [0, 0], "ex": [0, 0], "tr": [0, 0]}
    xfinal = [0] * NT

    def emit_batch(b, prev_ffn_end_tok, prev_store_toks, tok_bm, tok_gw):
        xload = [0] * NT
        for i in range(NT):
            if prev_store_toks is not None:
                p.wait("sync", so, prev_store_toks[i])
            xload[i] = p.emit("sync", lambda s, b=b, i=i: s.dma_start(
                XF[:, XR + 1024 * i:XR + 1024 * i + 1024], x_h[b, i]),
                inc=(sx, 16))

        if b == 0:
            tap("x0", XF[:, XR:XR + 8192], 8192, sx, xload[NT - 1], f32=True)
        xn_ready = emit_ln(b, xload, pre_sem=st, pre_tok=prev_ffn_end_tok)
        if b == 0:
            tap("xn", AB[:, XN:XN + 8192], 8192, ss, xn_ready[NT - 1])
            tap("lns", AB[:, LNS:LNS + 1024], 1024, sv, xn_ready[NT - 1])
            tap("stat", XF[:, STAT:STAT + 16], 16, sv, xn_ready[NT - 1],
                f32=True)

        attn_half_end = prev_ffn_end_tok or 0
        last_OL = 0
        attn_u_last = 0
        for h in range(2):
            tok_wt = emit_wt_load(h, attn_half_end)

            banks = [("Alo", psA, 0), ("Ahi", psA, 512),
                     ("Blo", psB, 0), ("Bhi", psB, 512)]
            kv_last = 0
            for jb in range(4):
                for fc in range(4):
                    key, ps, co = banks[(jb * 4 + fc) % 4]
                    p.bank_wait("tensor", key)
                    p.wait("tensor", swt[1], tok_wt[1])
                    p.wait("tensor", ss, xn_ready[4 * jb + 3])
                    tok_mm = None
                    for dc in range(ND):
                        fn = lambda t, ps=ps, co=co, dc=dc, fc=fc, jb=jb: \
                            t.matmul(ps[:, co:co + 512],
                                     AB[:, WKT + 512 * dc + 128 * fc:
                                        WKT + 512 * dc + 128 * fc + 128],
                                     AB[:, XN + 2048 * dc + 512 * jb:
                                        XN + 2048 * dc + 512 * jb + 512],
                                     start=(dc == 0), stop=(dc == ND - 1))
                        tok_mm = p.emit("tensor", fn,
                                        inc=(st, 1) if dc == ND - 1 else None)
                    p.wait("vector", st, tok_mm)
                    tok_ev = p.emit(
                        "vector", lambda v, ps=ps, co=co, fc=fc, jb=jb:
                        v.tensor_copy(
                            AB[:, KF + 2048 * fc + 512 * jb:
                               KF + 2048 * fc + 512 * jb + 512],
                            ps[:, co:co + 512]), inc=(sv, 1))
                    p.bank_set(key, sv, tok_ev)
                    kv_last = tok_ev
            if b == 0 and h == 0:
                tap("kf", AB[:, KF:KF + 8192], 8192, sv, kv_last)

            for i in range(NT):
                key, co = ("Clo", 0) if i % 2 == 0 else ("Chi", 512)
                p.bank_wait("tensor", key)
                p.wait("tensor", swt[2], tok_wt[2])
                p.wait("tensor", ss, xn_ready[i])
                tok_mm = None
                for dc in range(ND):
                    fn = lambda t, co=co, dc=dc, i=i: t.matmul(
                        psC[:, co:co + 512],
                        AB[:, XN + 2048 * dc + 128 * i:
                           XN + 2048 * dc + 128 * i + 128],
                        AB[:, WVT + 512 * dc:WVT + 512 * dc + 512],
                        start=(dc == 0), stop=(dc == ND - 1))
                    tok_mm = p.emit("tensor", fn,
                                    inc=(st, 1) if dc == ND - 1 else None)
                p.wait("vector", st, tok_mm)
                tok_ev = p.emit("vector", lambda v, co=co, i=i: v.tensor_copy(
                    AB[:, VF + 512 * i:VF + 512 * i + 512],
                    psC[:, co:co + 512]), inc=(sv, 1))
                p.bank_set(key, sv, tok_ev)
                kv_last = tok_ev
            if b == 0 and h == 0:
                tap("vf", AB[:, VF:VF + 8192], 8192, sv, kv_last)

            for ib in range(4):
                p.wait("tensor", sv, kv_last)
                p.wait("tensor", swt[0], tok_wt[0])
                p.wait("tensor", swt[3], tok_wt[3])
                tok_q = 0
                for fc in range(4):
                    p.bank_wait("tensor", "D")
                    tok_mm = None
                    for dc in range(ND):
                        fn = lambda t, fc=fc, dc=dc, ib=ib: t.matmul(
                            psD[:, 0:512],
                            AB[:, WQT + 512 * dc + 128 * fc:
                               WQT + 512 * dc + 128 * fc + 128],
                            AB[:, XN + 2048 * dc + 512 * ib:
                               XN + 2048 * dc + 512 * ib + 512],
                            start=(dc == 0), stop=(dc == ND - 1))
                        tok_mm = p.emit("tensor", fn,
                                        inc=(st, 1) if dc == ND - 1 else None)
                    p.wait("vector", st, tok_mm)
                    tok_q = p.emit("vector", lambda v, fc=fc: v.tensor_copy(
                        AB[:, QB + 512 * fc:QB + 512 * fc + 512],
                        psD[:, 0:512]), inc=(sv, 1))
                    p.bank_set("D", sv, tok_q)
                p.wait("tensor", sv, tok_q)
                if b == 0 and h == 0 and ib == 0:
                    tap("qb", AB[:, QB:QB + 2048], 2048, sv, tok_q)

                tok_oa_ev = 0
                for pr in range(4):
                    okey, oco = ("Clo", 0) if pr % 2 == 0 else ("Chi", 512)
                    tok_P = {}
                    tok_OL = {}
                    njc = 4 * ib + 4
                    for jc in range(njc):
                        u = jc - 4 * ib
                        off = 128 * u if u > 0 else 0
                        w = 512 - off
                        skey, sps = ("A", psA) if jc % 2 == 0 else ("B", psB)
                        p.bank_wait("tensor", skey)
                        sA = lambda t, sps=sps, jc=jc, pr=pr, off=off, w=w: \
                            t.matmul(sps[:, 0:w],
                                     AB[0:64, KF + 2048 * pr + 128 * jc:
                                        KF + 2048 * pr + 128 * jc + 128],
                                     AB[0:64, QB + 512 * pr + off:
                                        QB + 512 * pr + 512],
                                     start=True, stop=True,
                                     tile_position=(0, 0))
                        sB = lambda t, sps=sps, jc=jc, pr=pr, off=off, w=w: \
                            t.matmul(sps[:, 512:512 + w],
                                     AB[64:128, KF + 2048 * pr + 128 * jc:
                                        KF + 2048 * pr + 128 * jc + 128],
                                     AB[64:128, QB + 512 * pr + off:
                                        QB + 512 * pr + 512],
                                     start=True, stop=True,
                                     tile_position=(64, 0))
                        p.emit("tensor", sA)
                        tok_s = p.emit("tensor", sB, inc=(st, 1))
                        p.wait("scalar", st, tok_s)
                        tb = TT0 if jc % 2 == 0 else TT1
                        sin = sps.ap().rearrange(
                            "q (a t) -> q a t", a=2)[:, :, 0:w]
                        tout = XF[:, tb:tb + 1024].rearrange(
                            "q (a t) -> q a t", a=2)[:, :, 0:w]
                        tok_th = p.emit("scalar", lambda s, i_=sin, o=tout:
                                        s.activation(o, i_, AF.Tanh,
                                                     scale=SCALE / CLAMP),
                                        inc=(ss, 1))
                        p.bank_set(skey, ss, tok_th)
                        pg = jc % 2
                        p.wait("scalar", st, tok_OL.get(jc - 2, last_OL))
                        pin = XF[:, tb:tb + 1024].rearrange(
                            "q (a t) -> q a t", a=2)[:, :, 0:w]
                        pout = AB[:, PP + 1024 * pg:PP + 1024 * pg + 1024] \
                            .rearrange("q (a t) -> q a t", a=2)[:, :, 0:w]
                        tok_P[jc] = p.emit(
                            "scalar", lambda s, i_=pin, o=pout:
                            s.activation(o, i_, AF.Exp, scale=CLAMP),
                            inc=(ss, 1))
                        need_sem, need_tok = ss, tok_P[jc]
                        if u >= 0:
                            p.wait("vector", ss, tok_P[jc])
                            p.emit("vector", lambda v, pg=pg: v.tensor_mul(
                                AB[:, PP + 1024 * pg:PP + 1024 * pg + 128],
                                AB[:, PP + 1024 * pg:PP + 1024 * pg + 128],
                                tri[:]))
                            tok_m = p.emit("vector", lambda v, pg=pg:
                                           v.tensor_mul(
                                               AB[:, PP + 1024 * pg + 512:
                                                  PP + 1024 * pg + 640],
                                               AB[:, PP + 1024 * pg + 512:
                                                  PP + 1024 * pg + 640],
                                               tri[:]), inc=(sv, 1))
                            need_sem, need_tok = sv, tok_m
                        if b == 0 and h == 0 and ib == 0 and pr == 0 \
                                and jc == 0:
                            tap("pp", AB[:, PP:PP + 1024], 1024,
                                need_sem, need_tok)
                        p.wait("tensor", need_sem, need_tok)
                        if jc == 0:
                            p.bank_wait("tensor", okey)
                            p.bank_wait("tensor", "D")
                        first = jc == 0
                        hhA, hhB = 2 * pr, 2 * pr + 1
                        oA = lambda t, jc=jc, hh=hhA, pg=pg, off=off, w=w, \
                            oco=oco, first=first: t.matmul(
                            psC[0:64, oco + off:oco + off + w],
                            AB[:, VF + 512 * jc + 64 * hh:
                               VF + 512 * jc + 64 * hh + 64],
                            AB[:, PP + 1024 * pg:PP + 1024 * pg + w],
                            start=first, stop=True, tile_position=(0, 0),
                            skip_group_check=True)
                        oB = lambda t, jc=jc, hh=hhB, pg=pg, off=off, w=w, \
                            oco=oco, first=first: t.matmul(
                            psC[64:128, oco + off:oco + off + w],
                            AB[:, VF + 512 * jc + 64 * hh:
                               VF + 512 * jc + 64 * hh + 64],
                            AB[:, PP + 1024 * pg + 512:
                               PP + 1024 * pg + 512 + w],
                            start=first, stop=True, tile_position=(0, 64),
                            skip_group_check=True)
                        lA = lambda t, pg=pg, off=off, w=w, first=first: \
                            t.matmul(psD[0:64, off:off + w], ones64[:],
                                     AB[:, PP + 1024 * pg:PP + 1024 * pg + w],
                                     start=first, stop=True,
                                     tile_position=(0, 0),
                                     skip_group_check=True)
                        lB = lambda t, pg=pg, off=off, w=w, first=first: \
                            t.matmul(psD[64:128, off:off + w], ones64[:],
                                     AB[:, PP + 1024 * pg + 512:
                                        PP + 1024 * pg + 512 + w],
                                     start=first, stop=True,
                                     tile_position=(0, 64),
                                     skip_group_check=True)
                        p.emit("tensor", oA)
                        p.emit("tensor", oB)
                        p.emit("tensor", lA)
                        tok_OL[jc] = p.emit("tensor", lB, inc=(st, 1))
                    last_OL = tok_OL[njc - 1]
                    p.wait("vector", st, last_OL)
                    p.emit("vector", lambda v: v.reciprocal(
                        XF[:, REC:REC + 512], psD[:, 0:512]))
                    p.emit("vector", lambda v: v.drain(fusable=False))
                    p.emit("vector", lambda v, oco=oco, pr=pr: v.tensor_mul(
                        AB[0:64, OA + 512 * pr:OA + 512 * pr + 512],
                        psC[0:64, oco:oco + 512],
                        XF[0:64, REC:REC + 512]))
                    tok_oa = p.emit("vector", lambda v, oco=oco, pr=pr:
                                    v.tensor_mul(
                                        AB[64:128, OA + 512 * pr:
                                           OA + 512 * pr + 512],
                                        psC[64:128, oco:oco + 512],
                                        XF[64:128, REC:REC + 512]),
                                    inc=(sv, 1))
                    p.bank_set(okey, sv, tok_oa)
                    p.bank_set("D", sv, tok_oa)
                    tok_oa_ev = tok_oa
                    if b == 0 and h == 0 and ib == 0 and pr == 0:
                        tap("oa", AB[:, OA:OA + 512], 512, sv, tok_oa)

                p.wait("tensor", sv, tok_oa_ev)
                ukeys = [("D", psD, 0), ("Alo", psA, 0)]
                for ii in range(4):
                    for dd in range(2):
                        key, ps, co = ukeys[(ii * 2 + dd) % 2]
                        p.bank_wait("tensor", key)
                        tok_mm = None
                        for fc in range(4):
                            fn = lambda t, ps=ps, co=co, fc=fc, ii=ii, dd=dd: \
                                t.matmul(
                                    ps[:, co:co + 512],
                                    AB[:, OA + 512 * fc + 128 * ii:
                                       OA + 512 * fc + 128 * ii + 128],
                                    AB[:, WOT + 1024 * fc + 512 * dd:
                                       WOT + 1024 * fc + 512 * dd + 512],
                                    start=(fc == 0), stop=(fc == 3))
                            tok_mm = p.emit("tensor", fn,
                                            inc=(st, 1) if fc == 3 else None)
                        p.wait("vector", st, tok_mm)
                        xs = XF[:, XR + 1024 * (4 * ib + ii) + 512 * dd:
                                XR + 1024 * (4 * ib + ii) + 512 * dd + 512]
                        tok_u = p.emit("vector", lambda v, xs=xs, ps=ps,
                                       co=co: v.tensor_add(
                                           xs, xs, ps[:, co:co + 512]),
                                       inc=(sv, 1))
                        p.bank_set(key, sv, tok_u)
                        attn_u_last = tok_u
            attn_half_end = st.n

        attn_end_st = st.n
        attn_end_sv = attn_u_last
        if b == 0:
            tap("xattn", XF[:, XR:XR + 8192], 8192, sv, attn_end_sv,
                f32=True)

        xn2_ready = emit_ln(b, None, pre_sem=sv, pre_tok=attn_end_sv)
        if b == 0:
            tap("xn2", AB[:, XN:XN + 8192], 8192, ss, xn2_ready[NT - 1])

        gate_done = [0] * NT
        p.bank_wait("tensor", "D")
        p.wait("tensor", sc, tok_gw)
        tok_gmm = 0
        for i in range(NT):
            p.wait("tensor", ss, xn2_ready[i])
            for dc in range(ND):
                fn = lambda t, dc=dc, i=i: t.matmul(
                    psD[:, 8 * i:8 * i + 8],
                    AB[:, XN + 2048 * dc + 128 * i:
                       XN + 2048 * dc + 128 * i + 128],
                    gwS[:, 8 * dc:8 * dc + 8],
                    start=(dc == 0), stop=(dc == ND - 1))
                tok_gmm = p.emit("tensor", fn,
                                 inc=(st, 1) if dc == ND - 1 else None)
        p.wait("vector", st, tok_gmm)
        gbbb = bass.AP(XF.ap().tensor, GBB, [[XF_COLS, 128], [0, NT], [1, 8]])
        pdv = psD[:, 0:128].rearrange("q (i e) -> q i e", e=8)
        glv = XF[:, GL:GL + 128].rearrange("q (i e) -> q i e", e=8)
        tok_lg = p.emit("vector", lambda v: v.tensor_tensor(
            glv, pdv, gbbb, ALU.add), inc=(sv, 1))
        p.bank_set("D", sv, tok_lg)
        p.emit("vector", lambda v: v.drain(fusable=False))
        tok_nm = p.emit("vector", lambda v: v.reduce_max(
            XF[:, SGL:SGL + 16], glv, AX.X, negate=True), inc=(sv, 1))
        p.wait("scalar", sv, tok_nm)
        tok_ex = 0
        for i in range(NT):
            tok_ex = p.emit("scalar", lambda s, i=i: s.activation(
                XF[:, PEX + 8 * i:PEX + 8 * i + 8],
                XF[:, GL + 8 * i:GL + 8 * i + 8], AF.Exp,
                bias=XF[:, SGL + i:SGL + i + 1],
                accum_out=XF[:, SGL + 16 + i:SGL + 17 + i]),
                inc=(ss, 1) if i == NT - 1 else None)
        p.wait("vector", ss, tok_ex)

        def gview(base, g):
            return XF[:, base + 64 * g:base + 64 * g + 64].rearrange(
                "q (i e) -> q i e", e=8)

        def sview(base, g):
            return XF[:, base + 8 * g:base + 8 * g + 8]

        def sbc(base, g):
            return bass.AP(XF.ap().tensor, base + 8 * g,
                           [[XF_COLS, 128], [1, 8], [0, 8]])

        for g in range(2):
            p.emit("vector", lambda v, g=g: v.reduce_max(
                sview(SGL + 48, g), gview(PEX, g), AX.X))
        p.emit("vector", lambda v: v.reciprocal(
            XF[:, SGL + 32:SGL + 48], XF[:, SGL + 16:SGL + 32]))
        for g in range(2):
            p.emit("vector", lambda v, g=g: v.tensor_tensor(
                gview(M1, g), gview(PEX, g), sbc(SGL + 48, g), ALU.is_ge))
        for g in range(2):
            p.emit("vector", lambda v, g=g: v.scalar_tensor_tensor(
                gview(T2, g), gview(M1, g), -BIG, gview(PEX, g),
                ALU.mult, ALU.add))
        for g in range(2):
            p.emit("vector", lambda v, g=g: v.reduce_max(
                sview(SGL + 64, g), gview(T2, g), AX.X))
        for g in range(2):
            p.emit("vector", lambda v, g=g: v.tensor_tensor(
                gview(T2, g), gview(T2, g), sbc(SGL + 64, g), ALU.is_ge))
        for g in range(2):
            p.emit("vector", lambda v, g=g: v.tensor_tensor(
                gview(M1, g), gview(M1, g), gview(T2, g), ALU.add))
        for g in range(2):
            p.emit("vector", lambda v, g=g: v.tensor_tensor(
                gview(M1, g), gview(M1, g), gview(PEX, g), ALU.mult))
        for g in range(2):
            p.emit("vector", lambda v, g=g: v.tensor_tensor(
                gview(CAL, g), gview(M1, g), sbc(SGL + 32, g), ALU.mult))
        p.emit("vector", lambda v: v.drain(fusable=False))
        tok_cb = p.emit("vector", lambda v: v.tensor_copy(
            cb8[:], XF[:, CAL:CAL + 128]), inc=(sv, 1))
        if b == 0:
            tap("sgl", XF[:, SGL:SGL + 80], 80, sv, tok_cb, f32=True)
            tap("m1", XF[:, M1:M1 + 128], 128, sv, tok_cb, f32=True)
        p.wait("tensor", sv, tok_cb)
        tok_gd = 0
        for g in range(2):
            p.bank_wait("tensor", "T")
            tok_tr = None
            for t_ in range(8):
                i = 8 * g + t_
                fn = lambda t, i=i, t_=t_: t.transpose(
                    psT[0:8, 128 * t_:128 * t_ + 128],
                    cb8[:, 8 * i:8 * i + 8], ident[:])
                tok_tr = p.emit("tensor", fn,
                                inc=(st, 1) if t_ == 7 else None)
            p.wait("vector", st, tok_tr)
            tok_gd = p.emit("vector", lambda v, g=g: v.tensor_copy(
                bct[0:8, 1024 * g:1024 * g + 1024], psT[0:8, 0:1024]),
                inc=(sv, 1))
            p.bank_set("T", sv, tok_gd)
        for i in range(NT):
            gate_done[i] = tok_gd

        hkeys = [("Alo", psA, 0), ("Ahi", psA, 512), ("Blo", psB, 0)]
        obanks = [[("Clo", psC, 0), ("Chi", psC, 512)],
                  [("D", psD, 0), ("Bhi", psB, 512)]]
        tok_wload = {}
        tok_expert_done = {}
        tok_g_read = {}

        def emit_wdma(k):
            j = k % 2
            p.wait("gpsimd", st, tok_expert_done.get(k - 2, attn_end_st))
            if k == 0:
                srcs = [(sw1d.ap(), W1BUF[j]), (sw2d.ap(), W2BUF[j])]
            else:
                srcs = [(w1d.ap()[k - 1], W1BUF[j]),
                        (w2d.ap()[k - 1], W2BUF[j])]
            tk = 0
            for src, dsto in srcs:
                s2 = src.rearrange("(c q) f -> q c f", q=128)
                d2 = AB[:, dsto:dsto + 8192].rearrange(
                    "q (c f) -> q c f", c=8)
                tk = p.emit("gpsimd", lambda g, s=s2, d=d2: g.dma_start(
                    d, s), inc=(swe[j], 16))
            tok_wload[k] = tk

        emit_wdma(0)
        emit_wdma(1)
        if b == 0:
            tap("w1c", AB[:, KF:KF + 8192], 8192, swe[0], tok_wload[0])
        tok_gelu = {}

        def emit_h(k, blk):
            j = k % 2
            p.wait("tensor", swe[j], tok_wload[k])
            gp = blk % 2
            gbase = GBUF[gp]
            p.wait("tensor", ss, xn2_ready[4 * blk + 3])
            gate_tok = tok_g_read.get(gp)
            tok_gl = None
            for ffc in range(ND):
                    key, ps, co = hkeys[ffc % 3]
                    p.bank_wait("tensor", key)
                    tok_mm = None
                    for dc in range(ND):
                        fn = lambda t, ps=ps, co=co, dc=dc, ffc=ffc, blk=blk, \
                            j=j: t.matmul(
                            ps[:, co:co + 512],
                            AB[:, W1BUF[j] + 1024 * dc + 128 * ffc:
                               W1BUF[j] + 1024 * dc + 128 * ffc + 128],
                            AB[:, XN + 2048 * dc + 512 * blk:
                               XN + 2048 * dc + 512 * blk + 512],
                            start=(dc == 0), stop=(dc == ND - 1))
                        tok_mm = p.emit("tensor", fn,
                                        inc=(st, 1) if dc == ND - 1 else None)
                    if (DBG == "h0" and b == 0 and k == 0 and blk == 0
                            and ffc == 0):
                        p.wait("vector", st, tok_mm)
                        tok_h0 = p.emit("vector", lambda v, ps=ps, co=co:
                                        v.tensor_copy(
                                            XF[:, TMP:TMP + 512],
                                            ps[:, co:co + 512]), inc=(sv, 1))
                        tap("h0", XF[:, TMP:TMP + 512], 512, sv, tok_h0,
                            f32=True)
                    p.wait("scalar", st, tok_mm)
                    if gate_tok is not None:
                        p.wait("scalar", st, gate_tok)
                    if k == 0:
                        bias = sb1f[:, ffc:ffc + 1]
                    else:
                        bias = b1f[:, 8 * (k - 1) + ffc:8 * (k - 1) + ffc + 1]
                    p.wait("scalar", sc, tok_bm)
                    tok_gl = p.emit("scalar", lambda s, gb=gbase, ffc=ffc,
                                    ps=ps, co=co, bias=bias: s.activation(
                                        AB[:, gb + 512 * ffc:
                                           gb + 512 * ffc + 512],
                                        ps[:, co:co + 512], AF.Gelu,
                                        bias=bias), inc=(ss, 1))
                    p.bank_set(key, ss, tok_gl)
            tok_gelu[(k, blk)] = tok_gl

        def emit_out(k, blk):
            j = k % 2
            gp = blk % 2
            gbase = GBUF[gp]
            tok_gl = tok_gelu[(k, blk)]
            p.wait("tensor", ss, tok_gl)
            if b == 0 and k == 0 and blk == 0:
                tap("g", AB[:, GBUF[0]:GBUF[0] + 4096], 4096, ss, tok_gl)
            for ii in range(4):
                    i_t = 4 * blk + ii
                    obk = obanks[ii % 2]
                    for okey, _, _ in obk:
                        p.bank_wait("tensor", okey)
                    tok_out = None
                    for ffc in range(ND):
                        for dd in range(2):
                            okey, ops_, oco = obk[dd]
                            last = (k != 0 and ffc == ND - 1 and dd == 1)
                            fn = lambda t, ops_=ops_, oco=oco, ffc=ffc, \
                                ii=ii, dd=dd, gb=gbase, j=j, last=last: \
                                t.matmul(
                                ops_[:, oco:oco + 512],
                                AB[:, gb + 512 * ffc + 128 * ii:
                                   gb + 512 * ffc + 128 * ii + 128],
                                AB[:, W2BUF[j] + 1024 * ffc + 512 * dd:
                                   W2BUF[j] + 1024 * ffc + 512 * dd + 512],
                                start=(ffc == 0), stop=last)
                            tok_out = p.emit("tensor", fn,
                                             inc=(st, 1) if last else None)
                    if k == 0:
                        p.wait("tensor", sv, gate_done[i_t])
                        p.wait("tensor", sc, tok_bm)
                        for dd in range(2):
                            okey, ops_, oco = obk[dd]
                            fnb = lambda t, ops_=ops_, oco=oco, i_t=i_t, \
                                dd=dd: t.matmul(
                                ops_[:, oco:oco + 512],
                                bct[:, 128 * i_t:128 * i_t + 128],
                                bmat[:, 512 * dd:512 * dd + 512],
                                start=False, stop=False)
                            p.emit("tensor", fnb)
                            fns = lambda t, ops_=ops_, oco=oco, dd=dd, \
                                last=(dd == 1): t.matmul(
                                ops_[:, oco:oco + 512],
                                ones1[:, 0:128],
                                bmatS[:, 512 * dd:512 * dd + 512],
                                start=False, stop=last)
                            tok_out = p.emit("tensor", fns,
                                             inc=(st, 1) if dd == 1 else None)
                    p.wait("vector", st, tok_out)
                    for dd in range(2):
                        okey, ops_, oco = obk[dd]
                        xs = XF[:, XR + 1024 * i_t + 512 * dd:
                                XR + 1024 * i_t + 512 * dd + 512]
                        if k == 0:
                            tok_e = p.emit(
                                "vector", lambda v, xs=xs, ops_=ops_,
                                oco=oco: v.tensor_add(
                                    xs, xs, ops_[:, oco:oco + 512]),
                                inc=(sv, 1))
                        else:
                            cs = XF[:, CAL + 8 * i_t + (k - 1):
                                    CAL + 8 * i_t + k]
                            tok_e = p.emit(
                                "vector", lambda v, xs=xs, ops_=ops_,
                                oco=oco, cs=cs: v.scalar_tensor_tensor(
                                    xs, ops_[:, oco:oco + 512], cs, xs,
                                    ALU.mult, ALU.add), inc=(sv, 1))
                        p.bank_set(okey, sv, tok_e)
                        if k == 8:
                            xfinal[i_t] = tok_e
            tok_g_read[gp] = st.n

        stages = [(k, blk) for k in range(9) for blk in range(4)]
        emit_h(*stages[0])
        for idx, (k, blk) in enumerate(stages):
            if idx + 1 < len(stages):
                emit_h(*stages[idx + 1])
            emit_out(k, blk)
            if blk == 3:
                tok_expert_done[k] = st.n
                if k + 2 <= 8:
                    emit_wdma(k + 2)
            if b == 0 and k == 0 and blk == 3:
                tap("xs0", XF[:, XR:XR + 8192], 8192, sv, sv.n, f32=True)

        store_toks = [0] * NT
        for i in range(NT):
            p.wait("sync", sv, xfinal[i])
            store_toks[i] = p.emit("sync", lambda s, b=b, i=i: s.dma_start(
                out_h[b, i], XF[:, XR + 1024 * i:XR + 1024 * i + 1024]),
                inc=(so, 16))
        return tok_expert_done[8], store_toks

    tok_bm, tok_gw = emit_constants()
    prev_end, prev_stores = None, None
    for b in range(NB):
        prev_end, prev_stores = emit_batch(b, prev_end, prev_stores,
                                           tok_bm, tok_gw)

    p.wait("sync", so, so.n)

    with nc.Block() as blk:
        @blk.sync
        def _(e):
            for fn in p.st["sync"]:
                fn(e)

        @blk.tensor
        def _(e):
            for fn in p.st["tensor"]:
                fn(e)

        @blk.vector
        def _(e):
            for fn in p.st["vector"]:
                fn(e)

        @blk.scalar
        def _(e):
            for fn in p.st["scalar"]:
                fn(e)

        @blk.gpsimd
        def _(e):
            for fn in p.st["gpsimd"]:
                fn(e)

    return nc


def kernel(**inputs):
    x = np.ascontiguousarray(inputs["L0_attn_x"])
    nc = build_program()
    b1 = np.asarray(inputs["L0_ffn_b1"])
    sb1 = np.asarray(inputs["L0_ffn_sb1"])
    gw = np.asarray(inputs["L0_ffn_gate_w"])
    base = {
        "wqt": np.ascontiguousarray(np.asarray(inputs["L0_attn_Wq"]).T),
        "wkt": np.ascontiguousarray(np.asarray(inputs["L0_attn_Wk"]).T),
        "wvt": np.ascontiguousarray(np.asarray(inputs["L0_attn_Wv"]).T),
        "wot": np.ascontiguousarray(np.asarray(inputs["L0_attn_Wo"]).T),
        "gwl": np.ascontiguousarray(
            gw.reshape(8, 128, 8).transpose(1, 0, 2).reshape(128, 64)),
        "gbv": np.ascontiguousarray(inputs["L0_ffn_gate_b"]),
        "sw1": np.ascontiguousarray(inputs["L0_ffn_sw1"]),
        "sb1l": np.ascontiguousarray(sb1.reshape(8, 128).T),
        "sw2": np.ascontiguousarray(inputs["L0_ffn_sw2"]),
        "sb2": np.ascontiguousarray(inputs["L0_ffn_sb2"]),
        "w1": np.ascontiguousarray(inputs["L0_ffn_w1"]),
        "b1l": np.ascontiguousarray(
            b1.reshape(8, 8, 128).transpose(2, 0, 1).reshape(128, 64)),
        "w2": np.ascontiguousarray(inputs["L0_ffn_w2"]),
        "b2": np.ascontiguousarray(inputs["L0_ffn_b2"]),
    }
    in_maps = []
    for c in range(8):
        m = dict(base)
        m["x"] = np.ascontiguousarray(x[NB * c:NB * c + NB])
        in_maps.append(m)
    res = run_bass_kernel_spmd(nc, in_maps, core_ids=list(range(8)))
    if os.environ.get("KDBG", ""):
        kernel.dbg = (res.results[0].get("dbgb"), res.results[0].get("dbgf"))
    return np.concatenate([r["out"] for r in res.results], axis=0)


if __name__ == "__main__":
    import time
    t0 = time.time()
    nc = build_program()
    print("built ok in %.1fs" % (time.time() - t0))


# version 07
# speedup vs baseline: 1.2930x; 1.2215x over previous
import os
import numpy as np
import concourse.bass as bass
import concourse.mybir as mybir
from concourse.bass_utils import run_bass_kernel_spmd

F32 = mybir.dt.float32
BF16 = mybir.dt.bfloat16
AF = mybir.ActivationFunctionType
ALU = mybir.AluOpType
AX = mybir.AxisListType

NB = 2
N = 2048
D = 1024
NT = N // 128
ND = D // 128
DH = 64
SCALE = DH ** -0.5
CLAMP = 50.0
NEXP = 8
BIG = 1e30
EPS = 1e-5

XN = 0
KF = 16384
VF = 24576
WQT = 32768
WKT = 36864
WVT = 40960
WOT = 45056
QB = 49152
OA = 51200
PP = 53248
GX = 55296
LNS = 57344
AB_COLS = 58368
W1BUF = [KF, WQT]
W2BUF = [VF, WVT]
GBUF = [QB, PP]

XR = 0
TT0 = 16384
TT1 = 17408
SQ = 18432
REC = 19456
CAL = 19968
STAT = 20096
GL = 20224
PEX = 20352
M1 = 20480
T2 = 20608
SGL = 20736
GBB = 20816
GBS = 20824
TMP = SQ
XF_COLS = 20832

OVL = {
    "A": ("A", "Alo", "Ahi"), "Alo": ("A", "Alo"), "Ahi": ("A", "Ahi"),
    "B": ("B", "Blo", "Bhi"), "Blo": ("B", "Blo"), "Bhi": ("B", "Bhi"),
    "Clo": ("Clo",), "Chi": ("Chi",), "D": ("D",), "T": ("T",),
    "LNS": ("LNS",),
}


class Sem:
    __slots__ = ("h", "n", "waited")

    def __init__(self, nc, name):
        self.h = nc.alloc_semaphore(name)
        self.n = 0
        self.waited = {}


class Prog:
    def __init__(self, nc):
        self.nc = nc
        self.st = {k: [] for k in ("sync", "tensor", "vector", "scalar",
                                   "gpsimd")}
        self.free = {}

    def emit(self, eng, fn, inc=None):
        if inc is None:
            self.st[eng].append(fn)
            return None
        sem, amt = inc
        sem.n += amt

        def play(e, fn=fn, h=sem.h, amt=amt):
            fn(e).then_inc(h, amt)

        self.st[eng].append(play)
        return sem.n

    def wait(self, eng, sem, tok):
        if tok is None or tok <= 0:
            return
        if sem.waited.get(eng, 0) >= tok:
            return
        sem.waited[eng] = tok
        self.st[eng].append(lambda e, h=sem.h, t=tok: e.wait_ge(h, t))

    def bank_wait(self, eng, key):
        for k in OVL[key]:
            if k in self.free:
                sem, tok = self.free[k]
                self.wait(eng, sem, tok)

    def bank_set(self, key, sem, tok):
        self.free[key] = (sem, tok)


def build_program():
    nc = bass.Bass("TRN2", target_bir_lowering=False)

    xd = nc.dram_tensor("x", [NB, N, D], F32, kind="ExternalInput")
    wqd = nc.dram_tensor("wqt", [D, D], F32, kind="ExternalInput")
    wkd = nc.dram_tensor("wkt", [D, D], F32, kind="ExternalInput")
    wvd = nc.dram_tensor("wvt", [D, D], F32, kind="ExternalInput")
    wod = nc.dram_tensor("wot", [D, D], F32, kind="ExternalInput")
    gwd = nc.dram_tensor("gwl", [128, 64], F32, kind="ExternalInput")
    gbd = nc.dram_tensor("gbv", [NEXP], F32, kind="ExternalInput")
    sw1d = nc.dram_tensor("sw1", [D, D], F32, kind="ExternalInput")
    sb1d = nc.dram_tensor("sb1l", [128, 8], F32, kind="ExternalInput")
    sw2d = nc.dram_tensor("sw2", [D, D], F32, kind="ExternalInput")
    sb2d = nc.dram_tensor("sb2", [D], F32, kind="ExternalInput")
    w1d = nc.dram_tensor("w1", [NEXP, D, D], F32, kind="ExternalInput")
    b1d = nc.dram_tensor("b1l", [128, 64], F32, kind="ExternalInput")
    w2d = nc.dram_tensor("w2", [NEXP, D, D], F32, kind="ExternalInput")
    b2d = nc.dram_tensor("b2", [NEXP, D], F32, kind="ExternalInput")
    outd = nc.dram_tensor("out", [NB, N, D], F32, kind="ExternalOutput")

    AB = nc.alloc_sbuf_tensor("AB", [128, AB_COLS], BF16)
    XF = nc.alloc_sbuf_tensor("XF", [128, XF_COLS], F32)
    ident = nc.alloc_sbuf_tensor("ident", [128, 128], BF16)
    tri = nc.alloc_sbuf_tensor("tri", [128, 128], BF16)
    ones64 = nc.alloc_sbuf_tensor("ones64", [128, 64], BF16)
    ones1 = nc.alloc_sbuf_tensor("ones1", [1, 128], BF16)
    onesP = nc.alloc_sbuf_tensor("onesP", [128, 128], BF16)
    gbhl = nc.alloc_sbuf_tensor("gbhl", [1, 16], BF16)
    gwS = nc.alloc_sbuf_tensor("gwS", [128, 64], BF16)
    b1f = nc.alloc_sbuf_tensor("b1f", [128, 64], F32)
    sb1f = nc.alloc_sbuf_tensor("sb1f", [128, 8], F32)
    bmat = nc.alloc_sbuf_tensor("bmat", [8, D], BF16)
    bmatS = nc.alloc_sbuf_tensor("bmatS", [1, D], BF16)
    bct = nc.alloc_sbuf_tensor("bct", [8, 2048], BF16)
    cb8 = nc.alloc_sbuf_tensor("cb8", [128, 128], BF16)
    cst = nc.alloc_sbuf_tensor("cst", [128, 4], F32)
    nc.const_aps.register(F32, EPS, cst[:, 0:1])
    nc.const_aps.register(F32, SCALE / CLAMP, cst[:, 1:2])
    nc.const_aps.register(F32, CLAMP, cst[:, 2:3])

    psA = nc.alloc_psum_tensor("psA", [128, 1024], F32)
    psB = nc.alloc_psum_tensor("psB", [128, 1024], F32)
    psC = nc.alloc_psum_tensor("psC", [128, 1024], F32)
    psD = nc.alloc_psum_tensor("psD", [128, 512], F32)
    psT = nc.alloc_psum_tensor("psT", [128, 1024], BF16)

    DBG = os.environ.get("KDBG", "")
    if DBG:
        dbgb = nc.dram_tensor("dbgb", [128, 8192], BF16,
                              kind="ExternalOutput")
        dbgf = nc.dram_tensor("dbgf", [128, 8192], F32,
                              kind="ExternalOutput")

    p = Prog(nc)
    sx = Sem(nc, "sx")
    sc = Sem(nc, "sc")
    swt = [Sem(nc, "swt%d" % i) for i in range(4)]
    swe = [Sem(nc, "swe0"), Sem(nc, "swe1")]
    sw = Sem(nc, "sw")
    so = Sem(nc, "so")
    st = Sem(nc, "st")
    sv = Sem(nc, "sv")
    ss = Sem(nc, "ss")
    sg = Sem(nc, "sg")

    def xb(col, n):
        return bass.AP(XF.ap().tensor, col, [[XF_COLS, 128], [0, n]])

    x_h = xd.ap().rearrange("b (i q) d -> b i q d", q=128)
    out_h = outd.ap().rearrange("b (i q) d -> b i q d", q=128)
    psT_v = psT.ap().rearrange("q (a t) -> q a t", a=ND)

    tapped = [False]

    def tap(name, src_ap, ncols, sem, tok, f32=False):
        if DBG != name or tapped[0]:
            return
        tapped[0] = True
        p.wait("sync", sem, tok)
        dst = (dbgf if f32 else dbgb).ap()[:, 0:ncols]
        tok_d = p.emit("sync", lambda s: s.dma_start(dst, src_ap),
                       inc=(so, 16))
        for eng in ("vector", "scalar", "tensor", "gpsimd"):
            p.wait(eng, so, tok_d)

    def emit_constants():
        p.emit("gpsimd", lambda g: g.memset(onesP[:], 1.0))
        p.emit("gpsimd", lambda g: g.memset(ones64[:], 1.0))
        p.emit("gpsimd", lambda g: g.memset(ones1[:], 1.0))
        p.emit("gpsimd", lambda g: g.affine_select(
            ident[:], onesP[:], [[-1, 128]], ALU.is_equal, 0.0,
            base=0, channel_multiplier=1))
        p.emit("gpsimd", lambda g: g.affine_select(
            tri[:], onesP[:], [[1, 128]], ALU.is_ge, 0.0,
            base=0, channel_multiplier=-1))
        p.emit("gpsimd", lambda g: g.memset(cst[:, 0:1], EPS))
        p.emit("gpsimd", lambda g: g.memset(cst[:, 1:2], SCALE / CLAMP))
        p.emit("gpsimd", lambda g: g.memset(cst[:, 2:3], CLAMP))
        tok_g = p.emit("gpsimd", lambda g: g.memset(cst[:, 3:4], 0.0),
                       inc=(sg, 1))
        p.wait("vector", sg, tok_g)
        sgtok[0] = tok_g
        p.emit("gpsimd", lambda g: g.dma_start(
            gwS[:], gwd.ap()), inc=(sc, 16))
        p.emit("gpsimd", lambda g: g.dma_start(bmat[:], b2d.ap()),
               inc=(sc, 16))
        p.emit("gpsimd", lambda g: g.dma_start(
            bmatS[:], sb2d.ap().rearrange("(a d) -> a d", a=1)),
            inc=(sc, 16))
        p.emit("sync", lambda s: s.dma_start(b1f[:], b1d.ap()),
               inc=(sc, 16))
        p.emit("sync", lambda s: s.dma_start(sb1f[:], sb1d.ap()),
               inc=(sc, 16))
        tok_gb = p.emit("sync", lambda s: s.dma_start(
            XF[0:1, GBS:GBS + 8],
            gbd.ap().rearrange("(a e) -> a e", a=1)), inc=(sc, 16))
        tok_bm = tok_gw = tok_gb
        p.wait("vector", sc, tok_gb)
        p.emit("vector", lambda v: v.tensor_copy(gbhl[0:1, 0:8],
                                                 XF[0:1, GBS:GBS + 8]))
        p.emit("vector", lambda v: v.drain(fusable=False))
        p.emit("vector", lambda v: v.scalar_tensor_tensor(
            XF[0:1, TMP:TMP + 8], gbhl[0:1, 0:8], -1.0,
            XF[0:1, GBS:GBS + 8], ALU.mult, ALU.add))
        p.emit("vector", lambda v: v.drain(fusable=False))
        tok_v = p.emit("vector", lambda v: v.tensor_copy(
            gbhl[0:1, 8:16], XF[0:1, TMP:TMP + 8]), inc=(sv, 1))
        p.wait("tensor", sv, tok_v)
        p.wait("tensor", sg, tok_g)
        p.emit("tensor", lambda t: t.matmul(
            psD[:, 0:8], ones1[:], gbhl[0:1, 0:8], start=True, stop=False))
        tok_t = p.emit("tensor", lambda t: t.matmul(
            psD[:, 0:8], ones1[:], gbhl[0:1, 8:16], start=False, stop=True),
            inc=(st, 1))
        p.wait("vector", st, tok_t)
        tok_v2 = p.emit("vector", lambda v: v.tensor_copy(
            XF[:, GBB:GBB + 8], psD[:, 0:8]), inc=(sv, 1))
        p.bank_set("D", sv, tok_v2)
        return tok_bm, tok_gw

    sgtok = [0]

    def emit_ln(b, xload_toks, pre_sem=None, pre_tok=None):
        xn_ready = [0] * NT
        tok_sq = 0
        for i in range(NT):
            if i == 0:
                if xload_toks is not None:
                    p.wait("vector", sx, xload_toks[NT - 1])
                    p.wait("scalar", sx, xload_toks[NT - 1])
                if pre_tok:
                    p.wait("vector", pre_sem, pre_tok)
                    p.wait("scalar", pre_sem, pre_tok)
            c = STAT + 8 * i
            xt = XF[:, XR + 1024 * i:XR + 1024 * i + 1024]
            p.emit("vector", lambda v, xt=xt, c=c: v.reduce_sum(
                XF[:, c:c + 1], xt, AX.X))
            tok_sq = p.emit("scalar", lambda s, xt=xt, c=c: s.activation(
                XF[:, SQ:SQ + 1024], xt, AF.Square,
                accum_out=XF[:, c + 1:c + 2]), inc=(ss, 1))
        p.wait("vector", ss, tok_sq)
        for i in range(NT):
            c = STAT + 8 * i
            p.emit("vector", lambda v, c=c: v.tensor_scalar_mul(
                XF[:, c + 3:c + 4], XF[:, c + 1:c + 2], 1.0 / D))
        for i in range(NT):
            c = STAT + 8 * i
            p.emit("vector", lambda v, c=c: v.tensor_scalar_mul(
                XF[:, c + 2:c + 3], XF[:, c:c + 1], 1.0 / D))
        for i in range(NT):
            c = STAT + 8 * i
            p.emit("vector", lambda v, c=c: v.tensor_mul(
                XF[:, c + 4:c + 5], XF[:, c + 2:c + 3], XF[:, c + 2:c + 3]))
        tok_var = 0
        for i in range(NT):
            c = STAT + 8 * i
            tok_var = p.emit("vector", lambda v, c=c: v.scalar_tensor_tensor(
                XF[:, c + 5:c + 6], XF[:, c + 4:c + 5], -1.0,
                XF[:, c + 3:c + 4], ALU.mult, ALU.add),
                inc=(sv, 1) if i == NT - 1 else None)
        p.wait("scalar", sv, tok_var)
        p.wait("scalar", sg, sgtok[0])
        tok_sd = 0
        for i in range(NT):
            c = STAT + 8 * i
            tok_sd = p.emit("scalar", lambda s, c=c: s.activation(
                XF[:, c + 6:c + 7], XF[:, c + 5:c + 6], AF.Sqrt, bias=EPS),
                inc=(ss, 1) if i == NT - 1 else None)
        p.wait("vector", ss, tok_sd)
        for i in range(NT):
            c = STAT + 8 * i
            p.emit("vector", lambda v, c=c: v.reciprocal(
                XF[:, c + 7:c + 8], XF[:, c + 6:c + 7]))
        for i in range(NT):
            c = STAT + 8 * i
            xt = XF[:, XR + 1024 * i:XR + 1024 * i + 1024]
            p.bank_wait("vector", "LNS")
            p.emit("vector", lambda v, xt=xt, c=c: v.tensor_tensor(
                XF[:, TMP:TMP + 1024], xt, xb(c + 2, 1024), ALU.subtract))
            p.emit("vector", lambda v: v.drain(fusable=False))
            tok_xn = p.emit("vector", lambda v, c=c: v.tensor_tensor(
                AB[:, LNS:LNS + 1024], XF[:, TMP:TMP + 1024],
                xb(c + 7, 1024), ALU.mult), inc=(sv, 1))
            p.wait("tensor", sv, tok_xn)
            p.bank_wait("tensor", "T")
            tok_tr = None
            for cc in range(ND):
                fn = lambda t, cc=cc: t.transpose(
                    psT[:, 128 * cc:128 * cc + 128],
                    AB[:, LNS + 128 * cc:LNS + 128 * cc + 128], ident[:])
                tok_tr = p.emit("tensor", fn,
                                inc=(st, 1) if cc == ND - 1 else None)
            p.wait("scalar", st, tok_tr)
            dst = AB[:, XN:XN + 16384].rearrange(
                "q (dc t) -> q dc t", dc=ND)[:, :, 128 * i:128 * i + 128]
            tok_ev = p.emit("scalar", lambda s, dst=dst: s.copy(
                dst, psT_v), inc=(ss, 1))
            p.bank_set("T", ss, tok_ev)
            p.bank_set("LNS", ss, tok_ev)
            xn_ready[i] = tok_ev
        return xn_ready

    def emit_wt_load(h, area_free_tok):
        p.wait("gpsimd", st, area_free_tok)
        toks = {}
        for wi, (wd, dsto) in enumerate(((wqd, WQT), (wkd, WKT),
                                         (wvd, WVT))):
            src = wd.ap().rearrange(
                "(dc q) f -> q dc f", q=128)[:, :, 512 * h:512 * h + 512]
            dst = AB[:, dsto:dsto + 4096].rearrange(
                "q (dc f) -> q dc f", dc=ND)
            toks[wi] = p.emit("gpsimd", lambda g, s=src, d=dst:
                              g.dma_start(d, s), inc=(swt[wi], 16))
        srco = wod.ap().rearrange("(hh fc q) d -> hh q fc d", q=128, fc=4)[h]
        dsto_ = AB[:, WOT:WOT + 4096].rearrange("q (fc d) -> q fc d", fc=4)
        toks[3] = p.emit("gpsimd", lambda g, s=srco, d=dsto_:
                         g.dma_start(d, s), inc=(swt[3], 16))
        return toks

    gate_state = {"c": [0, 0], "ex": [0, 0], "tr": [0, 0]}
    xfinal = [0] * NT

    def emit_batch(b, prev_ffn_end_tok, prev_store_toks, tok_bm, tok_gw):
        xload = [0] * NT
        for i in range(NT):
            if prev_store_toks is not None:
                p.wait("sync", so, prev_store_toks[i])
            xload[i] = p.emit("sync", lambda s, b=b, i=i: s.dma_start(
                XF[:, XR + 1024 * i:XR + 1024 * i + 1024], x_h[b, i]),
                inc=(sx, 16))

        if b == 0:
            tap("x0", XF[:, XR:XR + 8192], 8192, sx, xload[NT - 1], f32=True)
        xn_ready = emit_ln(b, xload, pre_sem=st, pre_tok=prev_ffn_end_tok)
        if b == 0:
            tap("xn", AB[:, XN:XN + 8192], 8192, ss, xn_ready[NT - 1])
            tap("lns", AB[:, LNS:LNS + 1024], 1024, sv, xn_ready[NT - 1])
            tap("stat", XF[:, STAT:STAT + 16], 16, sv, xn_ready[NT - 1],
                f32=True)

        attn_half_end = prev_ffn_end_tok or 0
        last_OL = 0
        attn_u_last = 0
        for h in range(2):
            tok_wt = emit_wt_load(h, attn_half_end)

            banks = [("Alo", psA, 0), ("Ahi", psA, 512),
                     ("Blo", psB, 0), ("Bhi", psB, 512)]
            kv_last = 0
            for jb in range(4):
                for fc in range(4):
                    key, ps, co = banks[(jb * 4 + fc) % 4]
                    p.bank_wait("tensor", key)
                    p.wait("tensor", swt[1], tok_wt[1])
                    p.wait("tensor", ss, xn_ready[4 * jb + 3])
                    tok_mm = None
                    for dc in range(ND):
                        fn = lambda t, ps=ps, co=co, dc=dc, fc=fc, jb=jb: \
                            t.matmul(ps[:, co:co + 512],
                                     AB[:, WKT + 512 * dc + 128 * fc:
                                        WKT + 512 * dc + 128 * fc + 128],
                                     AB[:, XN + 2048 * dc + 512 * jb:
                                        XN + 2048 * dc + 512 * jb + 512],
                                     start=(dc == 0), stop=(dc == ND - 1))
                        tok_mm = p.emit("tensor", fn,
                                        inc=(st, 1) if dc == ND - 1 else None)
                    p.wait("vector", st, tok_mm)
                    tok_ev = p.emit(
                        "vector", lambda v, ps=ps, co=co, fc=fc, jb=jb:
                        v.tensor_copy(
                            AB[:, KF + 2048 * fc + 512 * jb:
                               KF + 2048 * fc + 512 * jb + 512],
                            ps[:, co:co + 512]), inc=(sv, 1))
                    p.bank_set(key, sv, tok_ev)
                    kv_last = tok_ev
            if b == 0 and h == 0:
                tap("kf", AB[:, KF:KF + 8192], 8192, sv, kv_last)

            for i in range(NT):
                key, co = ("Clo", 0) if i % 2 == 0 else ("Chi", 512)
                p.bank_wait("tensor", key)
                p.wait("tensor", swt[2], tok_wt[2])
                p.wait("tensor", ss, xn_ready[i])
                tok_mm = None
                for dc in range(ND):
                    fn = lambda t, co=co, dc=dc, i=i: t.matmul(
                        psC[:, co:co + 512],
                        AB[:, XN + 2048 * dc + 128 * i:
                           XN + 2048 * dc + 128 * i + 128],
                        AB[:, WVT + 512 * dc:WVT + 512 * dc + 512],
                        start=(dc == 0), stop=(dc == ND - 1))
                    tok_mm = p.emit("tensor", fn,
                                    inc=(st, 1) if dc == ND - 1 else None)
                p.wait("vector", st, tok_mm)
                tok_ev = p.emit("vector", lambda v, co=co, i=i: v.tensor_copy(
                    AB[:, VF + 512 * i:VF + 512 * i + 512],
                    psC[:, co:co + 512]), inc=(sv, 1))
                p.bank_set(key, sv, tok_ev)
                kv_last = tok_ev
            if b == 0 and h == 0:
                tap("vf", AB[:, VF:VF + 8192], 8192, sv, kv_last)

            for ib in range(4):
                p.wait("tensor", sv, kv_last)
                p.wait("tensor", swt[0], tok_wt[0])
                p.wait("tensor", swt[3], tok_wt[3])
                tok_q = 0
                for fc in range(4):
                    p.bank_wait("tensor", "D")
                    tok_mm = None
                    for dc in range(ND):
                        fn = lambda t, fc=fc, dc=dc, ib=ib: t.matmul(
                            psD[:, 0:512],
                            AB[:, WQT + 512 * dc + 128 * fc:
                               WQT + 512 * dc + 128 * fc + 128],
                            AB[:, XN + 2048 * dc + 512 * ib:
                               XN + 2048 * dc + 512 * ib + 512],
                            start=(dc == 0), stop=(dc == ND - 1))
                        tok_mm = p.emit("tensor", fn,
                                        inc=(st, 1) if dc == ND - 1 else None)
                    p.wait("vector", st, tok_mm)
                    tok_q = p.emit("vector", lambda v, fc=fc: v.tensor_copy(
                        AB[:, QB + 512 * fc:QB + 512 * fc + 512],
                        psD[:, 0:512]), inc=(sv, 1))
                    p.bank_set("D", sv, tok_q)
                p.wait("tensor", sv, tok_q)
                if b == 0 and h == 0 and ib == 0:
                    tap("qb", AB[:, QB:QB + 2048], 2048, sv, tok_q)

                tok_oa_ev = 0
                njc = 4 * ib + 4
                steps = [(pr, jc) for pr in range(4) for jc in range(njc)]
                nsteps = len(steps)
                tok_P = {}
                tok_OL = {}
                tok_need = {}
                for tg in range(nsteps + 1):
                    if tg < nsteps:
                        pr, jc = steps[tg]
                        u = jc - 4 * ib
                        off = 128 * u if u > 0 else 0
                        w = 512 - off
                        skey, sps = ("A", psA) if jc % 2 == 0 else ("B", psB)
                        p.bank_wait("tensor", skey)
                        sA = lambda t, sps=sps, jc=jc, pr=pr, off=off, w=w: \
                            t.matmul(sps[:, 0:w],
                                     AB[0:64, KF + 2048 * pr + 128 * jc:
                                        KF + 2048 * pr + 128 * jc + 128],
                                     AB[0:64, QB + 512 * pr + off:
                                        QB + 512 * pr + 512],
                                     start=True, stop=True,
                                     tile_position=(0, 0))
                        sB = lambda t, sps=sps, jc=jc, pr=pr, off=off, w=w: \
                            t.matmul(sps[:, 512:512 + w],
                                     AB[64:128, KF + 2048 * pr + 128 * jc:
                                        KF + 2048 * pr + 128 * jc + 128],
                                     AB[64:128, QB + 512 * pr + off:
                                        QB + 512 * pr + 512],
                                     start=True, stop=True,
                                     tile_position=(64, 0))
                        p.emit("tensor", sA)
                        tok_s = p.emit("tensor", sB, inc=(st, 1))
                        p.wait("scalar", st, tok_s)
                        tb = TT0 if jc % 2 == 0 else TT1
                        sin = sps.ap().rearrange(
                            "q (a t) -> q a t", a=2)[:, :, 0:w]
                        tout = XF[:, tb:tb + 1024].rearrange(
                            "q (a t) -> q a t", a=2)[:, :, 0:w]
                        tok_th = p.emit("scalar", lambda s, i_=sin, o=tout:
                                        s.activation(o, i_, AF.Tanh,
                                                     scale=SCALE / CLAMP),
                                        inc=(ss, 1))
                        p.bank_set(skey, ss, tok_th)
                        pg = jc % 2
                        p.wait("scalar", st, tok_OL.get(tg - 2, last_OL))
                        pin = XF[:, tb:tb + 1024].rearrange(
                            "q (a t) -> q a t", a=2)[:, :, 0:w]
                        pout = AB[:, PP + 1024 * pg:PP + 1024 * pg + 1024] \
                            .rearrange("q (a t) -> q a t", a=2)[:, :, 0:w]
                        tok_P[tg] = p.emit(
                            "scalar", lambda s, i_=pin, o=pout:
                            s.activation(o, i_, AF.Exp, scale=CLAMP),
                            inc=(ss, 1))
                        need_sem, need_tok = ss, tok_P[tg]
                        if u >= 0:
                            p.wait("vector", ss, tok_P[tg])
                            p.emit("vector", lambda v, pg=pg: v.tensor_mul(
                                AB[:, PP + 1024 * pg:PP + 1024 * pg + 128],
                                AB[:, PP + 1024 * pg:PP + 1024 * pg + 128],
                                tri[:]))
                            tok_m = p.emit("vector", lambda v, pg=pg:
                                           v.tensor_mul(
                                               AB[:, PP + 1024 * pg + 512:
                                                  PP + 1024 * pg + 640],
                                               AB[:, PP + 1024 * pg + 512:
                                                  PP + 1024 * pg + 640],
                                               tri[:]), inc=(sv, 1))
                            need_sem, need_tok = sv, tok_m
                        if b == 0 and h == 0 and ib == 0 and tg == 0:
                            tap("pp", AB[:, PP:PP + 1024], 1024,
                                need_sem, need_tok)
                        tok_need[tg] = (need_sem, need_tok)
                    if tg >= 1:
                        pr, jc = steps[tg - 1]
                        u = jc - 4 * ib
                        off = 128 * u if u > 0 else 0
                        w = 512 - off
                        pg = jc % 2
                        okey, oco = ("Clo", 0) if pr % 2 == 0 else ("Chi", 512)
                        need_sem, need_tok = tok_need[tg - 1]
                        p.wait("tensor", need_sem, need_tok)
                        if jc == 0:
                            p.bank_wait("tensor", okey)
                            p.bank_wait("tensor", "D")
                        first = jc == 0
                        hhA, hhB = 2 * pr, 2 * pr + 1
                        oA = lambda t, jc=jc, hh=hhA, pg=pg, off=off, w=w, \
                            oco=oco, first=first: t.matmul(
                            psC[0:64, oco + off:oco + off + w],
                            AB[:, VF + 512 * jc + 64 * hh:
                               VF + 512 * jc + 64 * hh + 64],
                            AB[:, PP + 1024 * pg:PP + 1024 * pg + w],
                            start=first, stop=True, tile_position=(0, 0),
                            skip_group_check=True)
                        oB = lambda t, jc=jc, hh=hhB, pg=pg, off=off, w=w, \
                            oco=oco, first=first: t.matmul(
                            psC[64:128, oco + off:oco + off + w],
                            AB[:, VF + 512 * jc + 64 * hh:
                               VF + 512 * jc + 64 * hh + 64],
                            AB[:, PP + 1024 * pg + 512:
                               PP + 1024 * pg + 512 + w],
                            start=first, stop=True, tile_position=(0, 64),
                            skip_group_check=True)
                        lA = lambda t, pg=pg, off=off, w=w, first=first: \
                            t.matmul(psD[0:64, off:off + w], ones64[:],
                                     AB[:, PP + 1024 * pg:PP + 1024 * pg + w],
                                     start=first, stop=True,
                                     tile_position=(0, 0),
                                     skip_group_check=True)
                        lB = lambda t, pg=pg, off=off, w=w, first=first: \
                            t.matmul(psD[64:128, off:off + w], ones64[:],
                                     AB[:, PP + 1024 * pg + 512:
                                        PP + 1024 * pg + 512 + w],
                                     start=first, stop=True,
                                     tile_position=(0, 64),
                                     skip_group_check=True)
                        p.emit("tensor", oA)
                        p.emit("tensor", oB)
                        p.emit("tensor", lA)
                        tok_OL[tg - 1] = p.emit("tensor", lB, inc=(st, 1))
                        if jc == njc - 1:
                            last_OL = tok_OL[tg - 1]
                            p.wait("vector", st, last_OL)
                            p.emit("vector", lambda v: v.reciprocal(
                                XF[:, REC:REC + 512], psD[:, 0:512]))
                            p.emit("vector", lambda v: v.drain(fusable=False))
                            p.emit("vector", lambda v, oco=oco, pr=pr:
                                   v.tensor_mul(
                                       AB[0:64, OA + 512 * pr:
                                          OA + 512 * pr + 512],
                                       psC[0:64, oco:oco + 512],
                                       XF[0:64, REC:REC + 512]))
                            tok_oa = p.emit("vector", lambda v, oco=oco,
                                            pr=pr: v.tensor_mul(
                                                AB[64:128, OA + 512 * pr:
                                                   OA + 512 * pr + 512],
                                                psC[64:128, oco:oco + 512],
                                                XF[64:128, REC:REC + 512]),
                                            inc=(sv, 1))
                            p.bank_set(okey, sv, tok_oa)
                            p.bank_set("D", sv, tok_oa)
                            tok_oa_ev = tok_oa
                            if b == 0 and h == 0 and ib == 0 and pr == 0:
                                tap("oa", AB[:, OA:OA + 512], 512, sv,
                                    tok_oa)

                p.wait("tensor", sv, tok_oa_ev)
                ukeys = [("D", psD, 0), ("Alo", psA, 0)]
                for ii in range(4):
                    for dd in range(2):
                        key, ps, co = ukeys[(ii * 2 + dd) % 2]
                        p.bank_wait("tensor", key)
                        tok_mm = None
                        for fc in range(4):
                            fn = lambda t, ps=ps, co=co, fc=fc, ii=ii, dd=dd: \
                                t.matmul(
                                    ps[:, co:co + 512],
                                    AB[:, OA + 512 * fc + 128 * ii:
                                       OA + 512 * fc + 128 * ii + 128],
                                    AB[:, WOT + 1024 * fc + 512 * dd:
                                       WOT + 1024 * fc + 512 * dd + 512],
                                    start=(fc == 0), stop=(fc == 3))
                            tok_mm = p.emit("tensor", fn,
                                            inc=(st, 1) if fc == 3 else None)
                        p.wait("vector", st, tok_mm)
                        xs = XF[:, XR + 1024 * (4 * ib + ii) + 512 * dd:
                                XR + 1024 * (4 * ib + ii) + 512 * dd + 512]
                        tok_u = p.emit("vector", lambda v, xs=xs, ps=ps,
                                       co=co: v.tensor_add(
                                           xs, xs, ps[:, co:co + 512]),
                                       inc=(sv, 1))
                        p.bank_set(key, sv, tok_u)
                        attn_u_last = tok_u
            attn_half_end = st.n

        attn_end_st = st.n
        attn_end_sv = attn_u_last
        if b == 0:
            tap("xattn", XF[:, XR:XR + 8192], 8192, sv, attn_end_sv,
                f32=True)

        xn2_ready = emit_ln(b, None, pre_sem=sv, pre_tok=attn_end_sv)
        if b == 0:
            tap("xn2", AB[:, XN:XN + 8192], 8192, ss, xn2_ready[NT - 1])

        gate_done = [0] * NT
        p.bank_wait("tensor", "D")
        p.wait("tensor", sc, tok_gw)
        tok_gmm = 0
        for i in range(NT):
            p.wait("tensor", ss, xn2_ready[i])
            for dc in range(ND):
                fn = lambda t, dc=dc, i=i: t.matmul(
                    psD[:, 8 * i:8 * i + 8],
                    AB[:, XN + 2048 * dc + 128 * i:
                       XN + 2048 * dc + 128 * i + 128],
                    gwS[:, 8 * dc:8 * dc + 8],
                    start=(dc == 0), stop=(dc == ND - 1))
                tok_gmm = p.emit("tensor", fn,
                                 inc=(st, 1) if dc == ND - 1 else None)
        p.wait("vector", st, tok_gmm)
        gbbb = bass.AP(XF.ap().tensor, GBB, [[XF_COLS, 128], [0, NT], [1, 8]])
        pdv = psD[:, 0:128].rearrange("q (i e) -> q i e", e=8)
        glv = XF[:, GL:GL + 128].rearrange("q (i e) -> q i e", e=8)
        tok_lg = p.emit("vector", lambda v: v.tensor_tensor(
            glv, pdv, gbbb, ALU.add), inc=(sv, 1))
        p.bank_set("D", sv, tok_lg)
        p.emit("vector", lambda v: v.drain(fusable=False))
        tok_nm = p.emit("vector", lambda v: v.reduce_max(
            XF[:, SGL:SGL + 16], glv, AX.X, negate=True), inc=(sv, 1))
        p.wait("scalar", sv, tok_nm)
        tok_ex = 0
        for i in range(NT):
            tok_ex = p.emit("scalar", lambda s, i=i: s.activation(
                XF[:, PEX + 8 * i:PEX + 8 * i + 8],
                XF[:, GL + 8 * i:GL + 8 * i + 8], AF.Exp,
                bias=XF[:, SGL + i:SGL + i + 1],
                accum_out=XF[:, SGL + 16 + i:SGL + 17 + i]),
                inc=(ss, 1) if i == NT - 1 else None)
        p.wait("vector", ss, tok_ex)

        def gview(base, g):
            return XF[:, base + 64 * g:base + 64 * g + 64].rearrange(
                "q (i e) -> q i e", e=8)

        def sview(base, g):
            return XF[:, base + 8 * g:base + 8 * g + 8]

        def sbc(base, g):
            return bass.AP(XF.ap().tensor, base + 8 * g,
                           [[XF_COLS, 128], [1, 8], [0, 8]])

        for g in range(2):
            p.emit("vector", lambda v, g=g: v.reduce_max(
                sview(SGL + 48, g), gview(PEX, g), AX.X))
        p.emit("vector", lambda v: v.reciprocal(
            XF[:, SGL + 32:SGL + 48], XF[:, SGL + 16:SGL + 32]))
        for g in range(2):
            p.emit("vector", lambda v, g=g: v.tensor_tensor(
                gview(M1, g), gview(PEX, g), sbc(SGL + 48, g), ALU.is_ge))
        for g in range(2):
            p.emit("vector", lambda v, g=g: v.scalar_tensor_tensor(
                gview(T2, g), gview(M1, g), -BIG, gview(PEX, g),
                ALU.mult, ALU.add))
        for g in range(2):
            p.emit("vector", lambda v, g=g: v.reduce_max(
                sview(SGL + 64, g), gview(T2, g), AX.X))
        for g in range(2):
            p.emit("vector", lambda v, g=g: v.tensor_tensor(
                gview(T2, g), gview(T2, g), sbc(SGL + 64, g), ALU.is_ge))
        for g in range(2):
            p.emit("vector", lambda v, g=g: v.tensor_tensor(
                gview(M1, g), gview(M1, g), gview(T2, g), ALU.add))
        for g in range(2):
            p.emit("vector", lambda v, g=g: v.tensor_tensor(
                gview(M1, g), gview(M1, g), gview(PEX, g), ALU.mult))
        for g in range(2):
            p.emit("vector", lambda v, g=g: v.tensor_tensor(
                gview(CAL, g), gview(M1, g), sbc(SGL + 32, g), ALU.mult))
        p.emit("vector", lambda v: v.drain(fusable=False))
        tok_cb = p.emit("vector", lambda v: v.tensor_copy(
            cb8[:], XF[:, CAL:CAL + 128]), inc=(sv, 1))
        if b == 0:
            tap("sgl", XF[:, SGL:SGL + 80], 80, sv, tok_cb, f32=True)
            tap("m1", XF[:, M1:M1 + 128], 128, sv, tok_cb, f32=True)
        p.wait("tensor", sv, tok_cb)
        tok_gd = 0
        for g in range(2):
            p.bank_wait("tensor", "T")
            tok_tr = None
            for t_ in range(8):
                i = 8 * g + t_
                fn = lambda t, i=i, t_=t_: t.transpose(
                    psT[0:8, 128 * t_:128 * t_ + 128],
                    cb8[:, 8 * i:8 * i + 8], ident[:])
                tok_tr = p.emit("tensor", fn,
                                inc=(st, 1) if t_ == 7 else None)
            p.wait("vector", st, tok_tr)
            tok_gd = p.emit("vector", lambda v, g=g: v.tensor_copy(
                bct[0:8, 1024 * g:1024 * g + 1024], psT[0:8, 0:1024]),
                inc=(sv, 1))
            p.bank_set("T", sv, tok_gd)
        for i in range(NT):
            gate_done[i] = tok_gd

        hkeys = [("Alo", psA, 0), ("Ahi", psA, 512), ("Blo", psB, 0)]
        obanks = [[("Clo", psC, 0), ("Chi", psC, 512)],
                  [("D", psD, 0), ("Bhi", psB, 512)]]
        tok_wload = {}
        tok_expert_done = {}
        tok_g_read = {}

        def emit_wdma(k):
            j = k % 2
            p.wait("gpsimd", st, tok_expert_done.get(k - 2, attn_end_st))
            if k == 0:
                srcs = [(sw1d.ap(), W1BUF[j]), (sw2d.ap(), W2BUF[j])]
            else:
                srcs = [(w1d.ap()[k - 1], W1BUF[j]),
                        (w2d.ap()[k - 1], W2BUF[j])]
            tk = 0
            for src, dsto in srcs:
                s2 = src.rearrange("(c q) f -> q c f", q=128)
                d2 = AB[:, dsto:dsto + 8192].rearrange(
                    "q (c f) -> q c f", c=8)
                tk = p.emit("gpsimd", lambda g, s=s2, d=d2: g.dma_start(
                    d, s), inc=(swe[j], 16))
            tok_wload[k] = tk

        emit_wdma(0)
        emit_wdma(1)
        if b == 0:
            tap("w1c", AB[:, KF:KF + 8192], 8192, swe[0], tok_wload[0])
        tok_gelu = {}

        def emit_h(k, blk):
            j = k % 2
            p.wait("tensor", swe[j], tok_wload[k])
            gp = blk % 2
            gbase = GBUF[gp]
            p.wait("tensor", ss, xn2_ready[4 * blk + 3])
            gate_tok = tok_g_read.get(gp)
            tok_gl = None
            for ffc in range(ND):
                    key, ps, co = hkeys[ffc % 3]
                    p.bank_wait("tensor", key)
                    tok_mm = None
                    for dc in range(ND):
                        fn = lambda t, ps=ps, co=co, dc=dc, ffc=ffc, blk=blk, \
                            j=j: t.matmul(
                            ps[:, co:co + 512],
                            AB[:, W1BUF[j] + 1024 * dc + 128 * ffc:
                               W1BUF[j] + 1024 * dc + 128 * ffc + 128],
                            AB[:, XN + 2048 * dc + 512 * blk:
                               XN + 2048 * dc + 512 * blk + 512],
                            start=(dc == 0), stop=(dc == ND - 1))
                        tok_mm = p.emit("tensor", fn,
                                        inc=(st, 1) if dc == ND - 1 else None)
                    if (DBG == "h0" and b == 0 and k == 0 and blk == 0
                            and ffc == 0):
                        p.wait("vector", st, tok_mm)
                        tok_h0 = p.emit("vector", lambda v, ps=ps, co=co:
                                        v.tensor_copy(
                                            XF[:, TMP:TMP + 512],
                                            ps[:, co:co + 512]), inc=(sv, 1))
                        tap("h0", XF[:, TMP:TMP + 512], 512, sv, tok_h0,
                            f32=True)
                    p.wait("scalar", st, tok_mm)
                    if gate_tok is not None:
                        p.wait("scalar", st, gate_tok)
                    if k == 0:
                        bias = sb1f[:, ffc:ffc + 1]
                    else:
                        bias = b1f[:, 8 * (k - 1) + ffc:8 * (k - 1) + ffc + 1]
                    p.wait("scalar", sc, tok_bm)
                    tok_gl = p.emit("scalar", lambda s, gb=gbase, ffc=ffc,
                                    ps=ps, co=co, bias=bias: s.activation(
                                        AB[:, gb + 512 * ffc:
                                           gb + 512 * ffc + 512],
                                        ps[:, co:co + 512], AF.Gelu,
                                        bias=bias), inc=(ss, 1))
                    p.bank_set(key, ss, tok_gl)
            tok_gelu[(k, blk)] = tok_gl

        def emit_out(k, blk):
            j = k % 2
            gp = blk % 2
            gbase = GBUF[gp]
            tok_gl = tok_gelu[(k, blk)]
            p.wait("tensor", ss, tok_gl)
            if b == 0 and k == 0 and blk == 0:
                tap("g", AB[:, GBUF[0]:GBUF[0] + 4096], 4096, ss, tok_gl)
            for ii in range(4):
                    i_t = 4 * blk + ii
                    obk = obanks[ii % 2]
                    for okey, _, _ in obk:
                        p.bank_wait("tensor", okey)
                    tok_out = None
                    for ffc in range(ND):
                        for dd in range(2):
                            okey, ops_, oco = obk[dd]
                            last = (k != 0 and ffc == ND - 1 and dd == 1)
                            fn = lambda t, ops_=ops_, oco=oco, ffc=ffc, \
                                ii=ii, dd=dd, gb=gbase, j=j, last=last: \
                                t.matmul(
                                ops_[:, oco:oco + 512],
                                AB[:, gb + 512 * ffc + 128 * ii:
                                   gb + 512 * ffc + 128 * ii + 128],
                                AB[:, W2BUF[j] + 1024 * ffc + 512 * dd:
                                   W2BUF[j] + 1024 * ffc + 512 * dd + 512],
                                start=(ffc == 0), stop=last)
                            tok_out = p.emit("tensor", fn,
                                             inc=(st, 1) if last else None)
                    if k == 0:
                        p.wait("tensor", sv, gate_done[i_t])
                        p.wait("tensor", sc, tok_bm)
                        for dd in range(2):
                            okey, ops_, oco = obk[dd]
                            fnb = lambda t, ops_=ops_, oco=oco, i_t=i_t, \
                                dd=dd: t.matmul(
                                ops_[:, oco:oco + 512],
                                bct[:, 128 * i_t:128 * i_t + 128],
                                bmat[:, 512 * dd:512 * dd + 512],
                                start=False, stop=False)
                            p.emit("tensor", fnb)
                            fns = lambda t, ops_=ops_, oco=oco, dd=dd, \
                                last=(dd == 1): t.matmul(
                                ops_[:, oco:oco + 512],
                                ones1[:, 0:128],
                                bmatS[:, 512 * dd:512 * dd + 512],
                                start=False, stop=last)
                            tok_out = p.emit("tensor", fns,
                                             inc=(st, 1) if dd == 1 else None)
                    p.wait("vector", st, tok_out)
                    for dd in range(2):
                        okey, ops_, oco = obk[dd]
                        xs = XF[:, XR + 1024 * i_t + 512 * dd:
                                XR + 1024 * i_t + 512 * dd + 512]
                        if k == 0:
                            tok_e = p.emit(
                                "vector", lambda v, xs=xs, ops_=ops_,
                                oco=oco: v.tensor_add(
                                    xs, xs, ops_[:, oco:oco + 512]),
                                inc=(sv, 1))
                        else:
                            cs = XF[:, CAL + 8 * i_t + (k - 1):
                                    CAL + 8 * i_t + k]
                            tok_e = p.emit(
                                "vector", lambda v, xs=xs, ops_=ops_,
                                oco=oco, cs=cs: v.scalar_tensor_tensor(
                                    xs, ops_[:, oco:oco + 512], cs, xs,
                                    ALU.mult, ALU.add), inc=(sv, 1))
                        p.bank_set(okey, sv, tok_e)
                        if k == 8:
                            xfinal[i_t] = tok_e
            tok_g_read[gp] = st.n

        stages = [(k, blk) for k in range(9) for blk in range(4)]
        emit_h(*stages[0])
        for idx, (k, blk) in enumerate(stages):
            if idx + 1 < len(stages):
                emit_h(*stages[idx + 1])
            emit_out(k, blk)
            if blk == 3:
                tok_expert_done[k] = st.n
                if k + 2 <= 8:
                    emit_wdma(k + 2)
            if b == 0 and k == 0 and blk == 3:
                tap("xs0", XF[:, XR:XR + 8192], 8192, sv, sv.n, f32=True)

        store_toks = [0] * NT
        for i in range(NT):
            p.wait("sync", sv, xfinal[i])
            store_toks[i] = p.emit("sync", lambda s, b=b, i=i: s.dma_start(
                out_h[b, i], XF[:, XR + 1024 * i:XR + 1024 * i + 1024]),
                inc=(so, 16))
        return tok_expert_done[8], store_toks

    tok_bm, tok_gw = emit_constants()
    prev_end, prev_stores = None, None
    for b in range(NB):
        prev_end, prev_stores = emit_batch(b, prev_end, prev_stores,
                                           tok_bm, tok_gw)

    p.wait("sync", so, so.n)

    with nc.Block() as blk:
        @blk.sync
        def _(e):
            for fn in p.st["sync"]:
                fn(e)

        @blk.tensor
        def _(e):
            for fn in p.st["tensor"]:
                fn(e)

        @blk.vector
        def _(e):
            for fn in p.st["vector"]:
                fn(e)

        @blk.scalar
        def _(e):
            for fn in p.st["scalar"]:
                fn(e)

        @blk.gpsimd
        def _(e):
            for fn in p.st["gpsimd"]:
                fn(e)

    return nc


def kernel(**inputs):
    x = np.ascontiguousarray(inputs["L0_attn_x"])
    nc = build_program()
    b1 = np.asarray(inputs["L0_ffn_b1"])
    sb1 = np.asarray(inputs["L0_ffn_sb1"])
    gw = np.asarray(inputs["L0_ffn_gate_w"])
    base = {
        "wqt": np.ascontiguousarray(np.asarray(inputs["L0_attn_Wq"]).T),
        "wkt": np.ascontiguousarray(np.asarray(inputs["L0_attn_Wk"]).T),
        "wvt": np.ascontiguousarray(np.asarray(inputs["L0_attn_Wv"]).T),
        "wot": np.ascontiguousarray(np.asarray(inputs["L0_attn_Wo"]).T),
        "gwl": np.ascontiguousarray(
            gw.reshape(8, 128, 8).transpose(1, 0, 2).reshape(128, 64)),
        "gbv": np.ascontiguousarray(inputs["L0_ffn_gate_b"]),
        "sw1": np.ascontiguousarray(inputs["L0_ffn_sw1"]),
        "sb1l": np.ascontiguousarray(sb1.reshape(8, 128).T),
        "sw2": np.ascontiguousarray(inputs["L0_ffn_sw2"]),
        "sb2": np.ascontiguousarray(inputs["L0_ffn_sb2"]),
        "w1": np.ascontiguousarray(inputs["L0_ffn_w1"]),
        "b1l": np.ascontiguousarray(
            b1.reshape(8, 8, 128).transpose(2, 0, 1).reshape(128, 64)),
        "w2": np.ascontiguousarray(inputs["L0_ffn_w2"]),
        "b2": np.ascontiguousarray(inputs["L0_ffn_b2"]),
    }
    in_maps = []
    for c in range(8):
        m = dict(base)
        m["x"] = np.ascontiguousarray(x[NB * c:NB * c + NB])
        in_maps.append(m)
    res = run_bass_kernel_spmd(nc, in_maps, core_ids=list(range(8)))
    if os.environ.get("KDBG", ""):
        kernel.dbg = (res.results[0].get("dbgb"), res.results[0].get("dbgf"))
    return np.concatenate([r["out"] for r in res.results], axis=0)


if __name__ == "__main__":
    import time
    t0 = time.time()
    nc = build_program()
    print("built ok in %.1fs" % (time.time() - t0))


# version 08 (final)
# speedup vs baseline: 1.2954x; 1.0018x over previous
import os
import numpy as np
import concourse.bass as bass
import concourse.mybir as mybir
from concourse.bass_utils import run_bass_kernel_spmd

F32 = mybir.dt.float32
BF16 = mybir.dt.bfloat16
AF = mybir.ActivationFunctionType
ALU = mybir.AluOpType
AX = mybir.AxisListType

NB = 2
N = 2048
D = 1024
NT = N // 128
ND = D // 128
DH = 64
SCALE = DH ** -0.5
CLAMP = 50.0
NEXP = 8
BIG = 1e30
EPS = 1e-5

XN = 0
KF = 16384
VF = 24576
WQT = 32768
WKT = 36864
WVT = 40960
WOT = 45056
QB = 49152
OA = 51200
PP = 53248
GX = 55296
LNS = 57344
AB_COLS = 58368
W1BUF = [KF, WQT]
W2BUF = [VF, WVT]
GBUF = [QB, PP]

XR = 0
TT0 = 16384
TT1 = 17408
SQ = 18432
REC = 19456
CAL = 19968
STAT = 20096
GL = 20224
PEX = 20352
M1 = 20480
T2 = 20608
SGL = 20736
GBB = 20816
GBS = 20824
TMP = SQ
XF_COLS = 20832

OVL = {
    "A": ("A", "Alo", "Ahi"), "Alo": ("A", "Alo"), "Ahi": ("A", "Ahi"),
    "B": ("B", "Blo", "Bhi"), "Blo": ("B", "Blo"), "Bhi": ("B", "Bhi"),
    "Clo": ("Clo",), "Chi": ("Chi",), "D": ("D",), "T": ("T",),
    "LNS": ("LNS",),
}


class Sem:
    __slots__ = ("h", "n", "waited")

    def __init__(self, nc, name):
        self.h = nc.alloc_semaphore(name)
        self.n = 0
        self.waited = {}


class Prog:
    def __init__(self, nc):
        self.nc = nc
        self.st = {k: [] for k in ("sync", "tensor", "vector", "scalar",
                                   "gpsimd")}
        self.free = {}

    def emit(self, eng, fn, inc=None):
        if inc is None:
            self.st[eng].append(fn)
            return None
        sem, amt = inc
        sem.n += amt

        def play(e, fn=fn, h=sem.h, amt=amt):
            fn(e).then_inc(h, amt)

        self.st[eng].append(play)
        return sem.n

    def wait(self, eng, sem, tok):
        if tok is None or tok <= 0:
            return
        if sem.waited.get(eng, 0) >= tok:
            return
        sem.waited[eng] = tok
        self.st[eng].append(lambda e, h=sem.h, t=tok: e.wait_ge(h, t))

    def bank_wait(self, eng, key):
        for k in OVL[key]:
            if k in self.free:
                sem, tok = self.free[k]
                self.wait(eng, sem, tok)

    def bank_set(self, key, sem, tok):
        self.free[key] = (sem, tok)


def build_program():
    nc = bass.Bass("TRN2", target_bir_lowering=False)

    xd = nc.dram_tensor("x", [NB, N, D], F32, kind="ExternalInput")
    wqd = nc.dram_tensor("wqt", [D, D], F32, kind="ExternalInput")
    wkd = nc.dram_tensor("wkt", [D, D], F32, kind="ExternalInput")
    wvd = nc.dram_tensor("wvt", [D, D], F32, kind="ExternalInput")
    wod = nc.dram_tensor("wot", [D, D], F32, kind="ExternalInput")
    gwd = nc.dram_tensor("gwl", [128, 64], F32, kind="ExternalInput")
    gbd = nc.dram_tensor("gbv", [NEXP], F32, kind="ExternalInput")
    sw1d = nc.dram_tensor("sw1", [D, D], F32, kind="ExternalInput")
    sb1d = nc.dram_tensor("sb1l", [128, 8], F32, kind="ExternalInput")
    sw2d = nc.dram_tensor("sw2", [D, D], F32, kind="ExternalInput")
    sb2d = nc.dram_tensor("sb2", [D], F32, kind="ExternalInput")
    w1d = nc.dram_tensor("w1", [NEXP, D, D], F32, kind="ExternalInput")
    b1d = nc.dram_tensor("b1l", [128, 64], F32, kind="ExternalInput")
    w2d = nc.dram_tensor("w2", [NEXP, D, D], F32, kind="ExternalInput")
    b2d = nc.dram_tensor("b2", [NEXP, D], F32, kind="ExternalInput")
    outd = nc.dram_tensor("out", [NB, N, D], F32, kind="ExternalOutput")

    AB = nc.alloc_sbuf_tensor("AB", [128, AB_COLS], BF16)
    XF = nc.alloc_sbuf_tensor("XF", [128, XF_COLS], F32)
    ident = nc.alloc_sbuf_tensor("ident", [128, 128], BF16)
    tri = nc.alloc_sbuf_tensor("tri", [128, 128], BF16)
    ones64 = nc.alloc_sbuf_tensor("ones64", [128, 64], BF16)
    ones1 = nc.alloc_sbuf_tensor("ones1", [1, 128], BF16)
    onesP = nc.alloc_sbuf_tensor("onesP", [128, 128], BF16)
    gbhl = nc.alloc_sbuf_tensor("gbhl", [1, 16], BF16)
    gwS = nc.alloc_sbuf_tensor("gwS", [128, 64], BF16)
    b1f = nc.alloc_sbuf_tensor("b1f", [128, 64], F32)
    sb1f = nc.alloc_sbuf_tensor("sb1f", [128, 8], F32)
    bmat = nc.alloc_sbuf_tensor("bmat", [8, D], BF16)
    bmatS = nc.alloc_sbuf_tensor("bmatS", [1, D], BF16)
    bct = nc.alloc_sbuf_tensor("bct", [8, 2048], BF16)
    cb8 = nc.alloc_sbuf_tensor("cb8", [128, 128], BF16)
    cst = nc.alloc_sbuf_tensor("cst", [128, 4], F32)
    nc.const_aps.register(F32, EPS, cst[:, 0:1])
    nc.const_aps.register(F32, SCALE / CLAMP, cst[:, 1:2])
    nc.const_aps.register(F32, CLAMP, cst[:, 2:3])

    psA = nc.alloc_psum_tensor("psA", [128, 1024], F32)
    psB = nc.alloc_psum_tensor("psB", [128, 1024], F32)
    psC = nc.alloc_psum_tensor("psC", [128, 1024], F32)
    psD = nc.alloc_psum_tensor("psD", [128, 512], F32)
    psT = nc.alloc_psum_tensor("psT", [128, 1024], BF16)

    DBG = os.environ.get("KDBG", "")
    if DBG:
        dbgb = nc.dram_tensor("dbgb", [128, 8192], BF16,
                              kind="ExternalOutput")
        dbgf = nc.dram_tensor("dbgf", [128, 8192], F32,
                              kind="ExternalOutput")

    p = Prog(nc)
    sx = Sem(nc, "sx")
    sc = Sem(nc, "sc")
    swt = [Sem(nc, "swt%d" % i) for i in range(4)]
    swe = [Sem(nc, "swe0"), Sem(nc, "swe1")]
    sw = Sem(nc, "sw")
    so = Sem(nc, "so")
    st = Sem(nc, "st")
    sv = Sem(nc, "sv")
    ss = Sem(nc, "ss")
    sg = Sem(nc, "sg")

    def xb(col, n):
        return bass.AP(XF.ap().tensor, col, [[XF_COLS, 128], [0, n]])

    x_h = xd.ap().rearrange("b (i q) d -> b i q d", q=128)
    out_h = outd.ap().rearrange("b (i q) d -> b i q d", q=128)
    psT_v = psT.ap().rearrange("q (a t) -> q a t", a=ND)

    tapped = [False]

    def tap(name, src_ap, ncols, sem, tok, f32=False):
        if DBG != name or tapped[0]:
            return
        tapped[0] = True
        p.wait("sync", sem, tok)
        dst = (dbgf if f32 else dbgb).ap()[:, 0:ncols]
        tok_d = p.emit("sync", lambda s: s.dma_start(dst, src_ap),
                       inc=(so, 16))
        for eng in ("vector", "scalar", "tensor", "gpsimd"):
            p.wait(eng, so, tok_d)

    def emit_constants():
        p.emit("gpsimd", lambda g: g.memset(onesP[:], 1.0))
        p.emit("gpsimd", lambda g: g.memset(ones64[:], 1.0))
        p.emit("gpsimd", lambda g: g.memset(ones1[:], 1.0))
        p.emit("gpsimd", lambda g: g.affine_select(
            ident[:], onesP[:], [[-1, 128]], ALU.is_equal, 0.0,
            base=0, channel_multiplier=1))
        p.emit("gpsimd", lambda g: g.affine_select(
            tri[:], onesP[:], [[1, 128]], ALU.is_ge, 0.0,
            base=0, channel_multiplier=-1))
        p.emit("gpsimd", lambda g: g.memset(cst[:, 0:1], EPS))
        p.emit("gpsimd", lambda g: g.memset(cst[:, 1:2], SCALE / CLAMP))
        p.emit("gpsimd", lambda g: g.memset(cst[:, 2:3], CLAMP))
        tok_g = p.emit("gpsimd", lambda g: g.memset(cst[:, 3:4], 0.0),
                       inc=(sg, 1))
        p.wait("vector", sg, tok_g)
        sgtok[0] = tok_g
        p.emit("gpsimd", lambda g: g.dma_start(
            gwS[:], gwd.ap()), inc=(sc, 16))
        p.emit("gpsimd", lambda g: g.dma_start(bmat[:], b2d.ap()),
               inc=(sc, 16))
        p.emit("gpsimd", lambda g: g.dma_start(
            bmatS[:], sb2d.ap().rearrange("(a d) -> a d", a=1)),
            inc=(sc, 16))
        p.emit("sync", lambda s: s.dma_start(b1f[:], b1d.ap()),
               inc=(sc, 16))
        p.emit("sync", lambda s: s.dma_start(sb1f[:], sb1d.ap()),
               inc=(sc, 16))
        tok_gb = p.emit("sync", lambda s: s.dma_start(
            XF[0:1, GBS:GBS + 8],
            gbd.ap().rearrange("(a e) -> a e", a=1)), inc=(sc, 16))
        tok_bm = tok_gw = tok_gb
        p.wait("vector", sc, tok_gb)
        p.emit("vector", lambda v: v.tensor_copy(gbhl[0:1, 0:8],
                                                 XF[0:1, GBS:GBS + 8]))
        p.emit("vector", lambda v: v.drain(fusable=False))
        p.emit("vector", lambda v: v.scalar_tensor_tensor(
            XF[0:1, TMP:TMP + 8], gbhl[0:1, 0:8], -1.0,
            XF[0:1, GBS:GBS + 8], ALU.mult, ALU.add))
        p.emit("vector", lambda v: v.drain(fusable=False))
        tok_v = p.emit("vector", lambda v: v.tensor_copy(
            gbhl[0:1, 8:16], XF[0:1, TMP:TMP + 8]), inc=(sv, 1))
        p.wait("tensor", sv, tok_v)
        p.wait("tensor", sg, tok_g)
        p.emit("tensor", lambda t: t.matmul(
            psD[:, 0:8], ones1[:], gbhl[0:1, 0:8], start=True, stop=False))
        tok_t = p.emit("tensor", lambda t: t.matmul(
            psD[:, 0:8], ones1[:], gbhl[0:1, 8:16], start=False, stop=True),
            inc=(st, 1))
        p.wait("vector", st, tok_t)
        tok_v2 = p.emit("vector", lambda v: v.tensor_copy(
            XF[:, GBB:GBB + 8], psD[:, 0:8]), inc=(sv, 1))
        p.bank_set("D", sv, tok_v2)
        return tok_bm, tok_gw

    sgtok = [0]

    def emit_ln(b, xload_toks, pre_sem=None, pre_tok=None):
        xn_ready = [0] * NT
        tok_sq = 0
        for i in range(NT):
            if i == 0:
                if xload_toks is not None:
                    p.wait("vector", sx, xload_toks[NT - 1])
                    p.wait("scalar", sx, xload_toks[NT - 1])
                if pre_tok:
                    p.wait("vector", pre_sem, pre_tok)
                    p.wait("scalar", pre_sem, pre_tok)
            c = STAT + 8 * i
            xt = XF[:, XR + 1024 * i:XR + 1024 * i + 1024]
            p.emit("vector", lambda v, xt=xt, c=c: v.reduce_sum(
                XF[:, c:c + 1], xt, AX.X))
            tok_sq = p.emit("scalar", lambda s, xt=xt, c=c: s.activation(
                XF[:, SQ:SQ + 1024], xt, AF.Square,
                accum_out=XF[:, c + 1:c + 2]), inc=(ss, 1))
        p.wait("vector", ss, tok_sq)
        for i in range(NT):
            c = STAT + 8 * i
            p.emit("vector", lambda v, c=c: v.tensor_scalar_mul(
                XF[:, c + 3:c + 4], XF[:, c + 1:c + 2], 1.0 / D))
        for i in range(NT):
            c = STAT + 8 * i
            p.emit("vector", lambda v, c=c: v.tensor_scalar_mul(
                XF[:, c + 2:c + 3], XF[:, c:c + 1], 1.0 / D))
        for i in range(NT):
            c = STAT + 8 * i
            p.emit("vector", lambda v, c=c: v.tensor_mul(
                XF[:, c + 4:c + 5], XF[:, c + 2:c + 3], XF[:, c + 2:c + 3]))
        tok_var = 0
        for i in range(NT):
            c = STAT + 8 * i
            tok_var = p.emit("vector", lambda v, c=c: v.scalar_tensor_tensor(
                XF[:, c + 5:c + 6], XF[:, c + 4:c + 5], -1.0,
                XF[:, c + 3:c + 4], ALU.mult, ALU.add),
                inc=(sv, 1) if i == NT - 1 else None)
        p.wait("scalar", sv, tok_var)
        p.wait("scalar", sg, sgtok[0])
        tok_sd = 0
        for i in range(NT):
            c = STAT + 8 * i
            tok_sd = p.emit("scalar", lambda s, c=c: s.activation(
                XF[:, c + 6:c + 7], XF[:, c + 5:c + 6], AF.Sqrt, bias=EPS),
                inc=(ss, 1) if i == NT - 1 else None)
        p.wait("vector", ss, tok_sd)
        for i in range(NT):
            c = STAT + 8 * i
            p.emit("vector", lambda v, c=c: v.reciprocal(
                XF[:, c + 7:c + 8], XF[:, c + 6:c + 7]))
        for i in range(NT):
            c = STAT + 8 * i
            xt = XF[:, XR + 1024 * i:XR + 1024 * i + 1024]
            p.bank_wait("vector", "LNS")
            p.emit("vector", lambda v, xt=xt, c=c: v.tensor_tensor(
                XF[:, TMP:TMP + 1024], xt, xb(c + 2, 1024), ALU.subtract))
            p.emit("vector", lambda v: v.drain(fusable=False))
            tok_xn = p.emit("vector", lambda v, c=c: v.tensor_tensor(
                AB[:, LNS:LNS + 1024], XF[:, TMP:TMP + 1024],
                xb(c + 7, 1024), ALU.mult), inc=(sv, 1))
            p.wait("tensor", sv, tok_xn)
            p.bank_wait("tensor", "T")
            tok_tr = None
            for cc in range(ND):
                fn = lambda t, cc=cc: t.transpose(
                    psT[:, 128 * cc:128 * cc + 128],
                    AB[:, LNS + 128 * cc:LNS + 128 * cc + 128], ident[:])
                tok_tr = p.emit("tensor", fn,
                                inc=(st, 1) if cc == ND - 1 else None)
            p.wait("scalar", st, tok_tr)
            dst = AB[:, XN:XN + 16384].rearrange(
                "q (dc t) -> q dc t", dc=ND)[:, :, 128 * i:128 * i + 128]
            tok_ev = p.emit("scalar", lambda s, dst=dst: s.copy(
                dst, psT_v), inc=(ss, 1))
            p.bank_set("T", ss, tok_ev)
            p.bank_set("LNS", ss, tok_ev)
            xn_ready[i] = tok_ev
        return xn_ready

    def emit_wt_load(h, area_free_tok):
        p.wait("gpsimd", st, area_free_tok)
        toks = {}
        for wi, (wd, dsto) in enumerate(((wqd, WQT), (wkd, WKT),
                                         (wvd, WVT))):
            src = wd.ap().rearrange(
                "(dc q) f -> q dc f", q=128)[:, :, 512 * h:512 * h + 512]
            dst = AB[:, dsto:dsto + 4096].rearrange(
                "q (dc f) -> q dc f", dc=ND)
            toks[wi] = p.emit("gpsimd", lambda g, s=src, d=dst:
                              g.dma_start(d, s), inc=(swt[wi], 16))
        srco = wod.ap().rearrange("(hh fc q) d -> hh q fc d", q=128, fc=4)[h]
        dsto_ = AB[:, WOT:WOT + 4096].rearrange("q (fc d) -> q fc d", fc=4)
        toks[3] = p.emit("gpsimd", lambda g, s=srco, d=dsto_:
                         g.dma_start(d, s), inc=(swt[3], 16))
        return toks

    gate_state = {"c": [0, 0], "ex": [0, 0], "tr": [0, 0]}
    xfinal = [0] * NT

    def emit_batch(b, prev_ffn_end_tok, prev_store_toks, tok_bm, tok_gw):
        xload = [0] * NT
        for i in range(NT):
            if prev_store_toks is not None:
                p.wait("sync", so, prev_store_toks[i])
            xload[i] = p.emit("sync", lambda s, b=b, i=i: s.dma_start(
                XF[:, XR + 1024 * i:XR + 1024 * i + 1024], x_h[b, i]),
                inc=(sx, 16))

        if b == 0:
            tap("x0", XF[:, XR:XR + 8192], 8192, sx, xload[NT - 1], f32=True)
        xn_ready = emit_ln(b, xload, pre_sem=st, pre_tok=prev_ffn_end_tok)
        if b == 0:
            tap("xn", AB[:, XN:XN + 8192], 8192, ss, xn_ready[NT - 1])
            tap("lns", AB[:, LNS:LNS + 1024], 1024, sv, xn_ready[NT - 1])
            tap("stat", XF[:, STAT:STAT + 16], 16, sv, xn_ready[NT - 1],
                f32=True)

        attn_half_end = prev_ffn_end_tok or 0
        last_OL = 0
        attn_u_last = 0
        for h in range(2):
            tok_wt = emit_wt_load(h, attn_half_end)

            banks = [("Alo", psA, 0), ("Ahi", psA, 512),
                     ("Blo", psB, 0), ("Bhi", psB, 512)]
            kv_last = 0
            for jb in range(4):
                for fc in range(4):
                    key, ps, co = banks[(jb * 4 + fc) % 4]
                    p.bank_wait("tensor", key)
                    p.wait("tensor", swt[1], tok_wt[1])
                    p.wait("tensor", ss, xn_ready[4 * jb + 3])
                    tok_mm = None
                    for dc in range(ND):
                        fn = lambda t, ps=ps, co=co, dc=dc, fc=fc, jb=jb: \
                            t.matmul(ps[:, co:co + 512],
                                     AB[:, WKT + 512 * dc + 128 * fc:
                                        WKT + 512 * dc + 128 * fc + 128],
                                     AB[:, XN + 2048 * dc + 512 * jb:
                                        XN + 2048 * dc + 512 * jb + 512],
                                     start=(dc == 0), stop=(dc == ND - 1))
                        tok_mm = p.emit("tensor", fn,
                                        inc=(st, 1) if dc == ND - 1 else None)
                    p.wait("vector", st, tok_mm)
                    tok_ev = p.emit(
                        "vector", lambda v, ps=ps, co=co, fc=fc, jb=jb:
                        v.tensor_copy(
                            AB[:, KF + 2048 * fc + 512 * jb:
                               KF + 2048 * fc + 512 * jb + 512],
                            ps[:, co:co + 512]), inc=(sv, 1))
                    p.bank_set(key, sv, tok_ev)
                    kv_last = tok_ev
            if b == 0 and h == 0:
                tap("kf", AB[:, KF:KF + 8192], 8192, sv, kv_last)

            for i in range(NT):
                key, co = ("Clo", 0) if i % 2 == 0 else ("Chi", 512)
                p.bank_wait("tensor", key)
                p.wait("tensor", swt[2], tok_wt[2])
                p.wait("tensor", ss, xn_ready[i])
                tok_mm = None
                for dc in range(ND):
                    fn = lambda t, co=co, dc=dc, i=i: t.matmul(
                        psC[:, co:co + 512],
                        AB[:, XN + 2048 * dc + 128 * i:
                           XN + 2048 * dc + 128 * i + 128],
                        AB[:, WVT + 512 * dc:WVT + 512 * dc + 512],
                        start=(dc == 0), stop=(dc == ND - 1))
                    tok_mm = p.emit("tensor", fn,
                                    inc=(st, 1) if dc == ND - 1 else None)
                p.wait("vector", st, tok_mm)
                tok_ev = p.emit("vector", lambda v, co=co, i=i: v.tensor_copy(
                    AB[:, VF + 512 * i:VF + 512 * i + 512],
                    psC[:, co:co + 512]), inc=(sv, 1))
                p.bank_set(key, sv, tok_ev)
                kv_last = tok_ev
            if b == 0 and h == 0:
                tap("vf", AB[:, VF:VF + 8192], 8192, sv, kv_last)

            for ib in range(4):
                p.wait("tensor", sv, kv_last)
                p.wait("tensor", swt[0], tok_wt[0])
                p.wait("tensor", swt[3], tok_wt[3])
                tok_q = 0
                for fc in range(4):
                    p.bank_wait("tensor", "D")
                    tok_mm = None
                    for dc in range(ND):
                        fn = lambda t, fc=fc, dc=dc, ib=ib: t.matmul(
                            psD[:, 0:512],
                            AB[:, WQT + 512 * dc + 128 * fc:
                               WQT + 512 * dc + 128 * fc + 128],
                            AB[:, XN + 2048 * dc + 512 * ib:
                               XN + 2048 * dc + 512 * ib + 512],
                            start=(dc == 0), stop=(dc == ND - 1))
                        tok_mm = p.emit("tensor", fn,
                                        inc=(st, 1) if dc == ND - 1 else None)
                    p.wait("vector", st, tok_mm)
                    tok_q = p.emit("vector", lambda v, fc=fc: v.tensor_copy(
                        AB[:, QB + 512 * fc:QB + 512 * fc + 512],
                        psD[:, 0:512]), inc=(sv, 1))
                    p.bank_set("D", sv, tok_q)
                p.wait("tensor", sv, tok_q)
                if b == 0 and h == 0 and ib == 0:
                    tap("qb", AB[:, QB:QB + 2048], 2048, sv, tok_q)

                tok_oa_ev = 0
                njc = 4 * ib + 4
                steps = [(pr, jc) for pr in range(4) for jc in range(njc)]
                nsteps = len(steps)
                tok_P = {}
                tok_OL = {}
                tok_need = {}
                for tg in range(nsteps + 1):
                    if tg < nsteps:
                        pr, jc = steps[tg]
                        u = jc - 4 * ib
                        off = 128 * u if u > 0 else 0
                        w = 512 - off
                        skey, sps = ("A", psA) if jc % 2 == 0 else ("B", psB)
                        p.bank_wait("tensor", skey)
                        sA = lambda t, sps=sps, jc=jc, pr=pr, off=off, w=w: \
                            t.matmul(sps[:, 0:w],
                                     AB[0:64, KF + 2048 * pr + 128 * jc:
                                        KF + 2048 * pr + 128 * jc + 128],
                                     AB[0:64, QB + 512 * pr + off:
                                        QB + 512 * pr + 512],
                                     start=True, stop=True,
                                     tile_position=(0, 0))
                        sB = lambda t, sps=sps, jc=jc, pr=pr, off=off, w=w: \
                            t.matmul(sps[:, 512:512 + w],
                                     AB[64:128, KF + 2048 * pr + 128 * jc:
                                        KF + 2048 * pr + 128 * jc + 128],
                                     AB[64:128, QB + 512 * pr + off:
                                        QB + 512 * pr + 512],
                                     start=True, stop=True,
                                     tile_position=(64, 0))
                        p.emit("tensor", sA)
                        tok_s = p.emit("tensor", sB, inc=(st, 1))
                        p.wait("scalar", st, tok_s)
                        tb = TT0 if jc % 2 == 0 else TT1
                        sin = sps.ap().rearrange(
                            "q (a t) -> q a t", a=2)[:, :, 0:w]
                        tout = XF[:, tb:tb + 1024].rearrange(
                            "q (a t) -> q a t", a=2)[:, :, 0:w]
                        tok_th = p.emit("scalar", lambda s, i_=sin, o=tout:
                                        s.activation(o, i_, AF.Tanh,
                                                     scale=SCALE / CLAMP),
                                        inc=(ss, 1))
                        p.bank_set(skey, ss, tok_th)
                        pg = tg % 4
                        p.wait("scalar", st, tok_OL.get(tg - 4, last_OL))
                        pin = XF[:, tb:tb + 1024].rearrange(
                            "q (a t) -> q a t", a=2)[:, :, 0:w]
                        pout = AB[:, PP + 1024 * pg:PP + 1024 * pg + 1024] \
                            .rearrange("q (a t) -> q a t", a=2)[:, :, 0:w]
                        tok_P[tg] = p.emit(
                            "scalar", lambda s, i_=pin, o=pout:
                            s.activation(o, i_, AF.Exp, scale=CLAMP),
                            inc=(ss, 1))
                        need_sem, need_tok = ss, tok_P[tg]
                        if u >= 0:
                            p.wait("vector", ss, tok_P[tg])
                            p.emit("vector", lambda v, pg=pg: v.tensor_mul(
                                AB[:, PP + 1024 * pg:PP + 1024 * pg + 128],
                                AB[:, PP + 1024 * pg:PP + 1024 * pg + 128],
                                tri[:]))
                            tok_m = p.emit("vector", lambda v, pg=pg:
                                           v.tensor_mul(
                                               AB[:, PP + 1024 * pg + 512:
                                                  PP + 1024 * pg + 640],
                                               AB[:, PP + 1024 * pg + 512:
                                                  PP + 1024 * pg + 640],
                                               tri[:]), inc=(sv, 1))
                            need_sem, need_tok = sv, tok_m
                        if b == 0 and h == 0 and ib == 0 and tg == 0:
                            tap("pp", AB[:, PP:PP + 1024], 1024,
                                need_sem, need_tok)
                        tok_need[tg] = (need_sem, need_tok)
                    if tg >= 1:
                        pr, jc = steps[tg - 1]
                        u = jc - 4 * ib
                        off = 128 * u if u > 0 else 0
                        w = 512 - off
                        pg = (tg - 1) % 4
                        okey, oco = ("Clo", 0) if pr % 2 == 0 else ("Chi", 512)
                        need_sem, need_tok = tok_need[tg - 1]
                        p.wait("tensor", need_sem, need_tok)
                        if jc == 0:
                            p.bank_wait("tensor", okey)
                            p.bank_wait("tensor", "D")
                        first = jc == 0
                        hhA, hhB = 2 * pr, 2 * pr + 1
                        oA = lambda t, jc=jc, hh=hhA, pg=pg, off=off, w=w, \
                            oco=oco, first=first: t.matmul(
                            psC[0:64, oco + off:oco + off + w],
                            AB[:, VF + 512 * jc + 64 * hh:
                               VF + 512 * jc + 64 * hh + 64],
                            AB[:, PP + 1024 * pg:PP + 1024 * pg + w],
                            start=first, stop=True, tile_position=(0, 0),
                            skip_group_check=True)
                        oB = lambda t, jc=jc, hh=hhB, pg=pg, off=off, w=w, \
                            oco=oco, first=first: t.matmul(
                            psC[64:128, oco + off:oco + off + w],
                            AB[:, VF + 512 * jc + 64 * hh:
                               VF + 512 * jc + 64 * hh + 64],
                            AB[:, PP + 1024 * pg + 512:
                               PP + 1024 * pg + 512 + w],
                            start=first, stop=True, tile_position=(0, 64),
                            skip_group_check=True)
                        lA = lambda t, pg=pg, off=off, w=w, first=first: \
                            t.matmul(psD[0:64, off:off + w], ones64[:],
                                     AB[:, PP + 1024 * pg:PP + 1024 * pg + w],
                                     start=first, stop=True,
                                     tile_position=(0, 0),
                                     skip_group_check=True)
                        lB = lambda t, pg=pg, off=off, w=w, first=first: \
                            t.matmul(psD[64:128, off:off + w], ones64[:],
                                     AB[:, PP + 1024 * pg + 512:
                                        PP + 1024 * pg + 512 + w],
                                     start=first, stop=True,
                                     tile_position=(0, 64),
                                     skip_group_check=True)
                        p.emit("tensor", oA)
                        p.emit("tensor", oB)
                        p.emit("tensor", lA)
                        tok_OL[tg - 1] = p.emit("tensor", lB, inc=(st, 1))
                        if jc == njc - 1:
                            last_OL = tok_OL[tg - 1]
                            p.wait("vector", st, last_OL)
                            p.emit("vector", lambda v: v.reciprocal(
                                XF[:, REC:REC + 512], psD[:, 0:512]))
                            p.emit("vector", lambda v: v.drain(fusable=False))
                            p.emit("vector", lambda v, oco=oco, pr=pr:
                                   v.tensor_mul(
                                       AB[0:64, OA + 512 * pr:
                                          OA + 512 * pr + 512],
                                       psC[0:64, oco:oco + 512],
                                       XF[0:64, REC:REC + 512]))
                            tok_oa = p.emit("vector", lambda v, oco=oco,
                                            pr=pr: v.tensor_mul(
                                                AB[64:128, OA + 512 * pr:
                                                   OA + 512 * pr + 512],
                                                psC[64:128, oco:oco + 512],
                                                XF[64:128, REC:REC + 512]),
                                            inc=(sv, 1))
                            p.bank_set(okey, sv, tok_oa)
                            p.bank_set("D", sv, tok_oa)
                            tok_oa_ev = tok_oa
                            if b == 0 and h == 0 and ib == 0 and pr == 0:
                                tap("oa", AB[:, OA:OA + 512], 512, sv,
                                    tok_oa)

                p.wait("tensor", sv, tok_oa_ev)
                ukeys = [("D", psD, 0), ("Alo", psA, 0)]
                for ii in range(4):
                    for dd in range(2):
                        key, ps, co = ukeys[(ii * 2 + dd) % 2]
                        p.bank_wait("tensor", key)
                        tok_mm = None
                        for fc in range(4):
                            fn = lambda t, ps=ps, co=co, fc=fc, ii=ii, dd=dd: \
                                t.matmul(
                                    ps[:, co:co + 512],
                                    AB[:, OA + 512 * fc + 128 * ii:
                                       OA + 512 * fc + 128 * ii + 128],
                                    AB[:, WOT + 1024 * fc + 512 * dd:
                                       WOT + 1024 * fc + 512 * dd + 512],
                                    start=(fc == 0), stop=(fc == 3))
                            tok_mm = p.emit("tensor", fn,
                                            inc=(st, 1) if fc == 3 else None)
                        p.wait("vector", st, tok_mm)
                        xs = XF[:, XR + 1024 * (4 * ib + ii) + 512 * dd:
                                XR + 1024 * (4 * ib + ii) + 512 * dd + 512]
                        tok_u = p.emit("vector", lambda v, xs=xs, ps=ps,
                                       co=co: v.tensor_add(
                                           xs, xs, ps[:, co:co + 512]),
                                       inc=(sv, 1))
                        p.bank_set(key, sv, tok_u)
                        attn_u_last = tok_u
            attn_half_end = st.n

        attn_end_st = st.n
        attn_end_sv = attn_u_last
        if b == 0:
            tap("xattn", XF[:, XR:XR + 8192], 8192, sv, attn_end_sv,
                f32=True)

        xn2_ready = emit_ln(b, None, pre_sem=sv, pre_tok=attn_end_sv)
        if b == 0:
            tap("xn2", AB[:, XN:XN + 8192], 8192, ss, xn2_ready[NT - 1])

        gate_done = [0] * NT
        p.bank_wait("tensor", "D")
        p.wait("tensor", sc, tok_gw)
        tok_gmm = 0
        for i in range(NT):
            p.wait("tensor", ss, xn2_ready[i])
            for dc in range(ND):
                fn = lambda t, dc=dc, i=i: t.matmul(
                    psD[:, 8 * i:8 * i + 8],
                    AB[:, XN + 2048 * dc + 128 * i:
                       XN + 2048 * dc + 128 * i + 128],
                    gwS[:, 8 * dc:8 * dc + 8],
                    start=(dc == 0), stop=(dc == ND - 1))
                tok_gmm = p.emit("tensor", fn,
                                 inc=(st, 1) if dc == ND - 1 else None)
        p.wait("vector", st, tok_gmm)
        gbbb = bass.AP(XF.ap().tensor, GBB, [[XF_COLS, 128], [0, NT], [1, 8]])
        pdv = psD[:, 0:128].rearrange("q (i e) -> q i e", e=8)
        glv = XF[:, GL:GL + 128].rearrange("q (i e) -> q i e", e=8)
        tok_lg = p.emit("vector", lambda v: v.tensor_tensor(
            glv, pdv, gbbb, ALU.add), inc=(sv, 1))
        p.bank_set("D", sv, tok_lg)
        p.emit("vector", lambda v: v.drain(fusable=False))
        tok_nm = p.emit("vector", lambda v: v.reduce_max(
            XF[:, SGL:SGL + 16], glv, AX.X, negate=True), inc=(sv, 1))
        p.wait("scalar", sv, tok_nm)
        tok_ex = 0
        for i in range(NT):
            tok_ex = p.emit("scalar", lambda s, i=i: s.activation(
                XF[:, PEX + 8 * i:PEX + 8 * i + 8],
                XF[:, GL + 8 * i:GL + 8 * i + 8], AF.Exp,
                bias=XF[:, SGL + i:SGL + i + 1],
                accum_out=XF[:, SGL + 16 + i:SGL + 17 + i]),
                inc=(ss, 1) if i == NT - 1 else None)
        p.wait("vector", ss, tok_ex)

        def gview(base, g):
            return XF[:, base + 64 * g:base + 64 * g + 64].rearrange(
                "q (i e) -> q i e", e=8)

        def sview(base, g):
            return XF[:, base + 8 * g:base + 8 * g + 8]

        def sbc(base, g):
            return bass.AP(XF.ap().tensor, base + 8 * g,
                           [[XF_COLS, 128], [1, 8], [0, 8]])

        for g in range(2):
            p.emit("vector", lambda v, g=g: v.reduce_max(
                sview(SGL + 48, g), gview(PEX, g), AX.X))
        p.emit("vector", lambda v: v.reciprocal(
            XF[:, SGL + 32:SGL + 48], XF[:, SGL + 16:SGL + 32]))
        for g in range(2):
            p.emit("vector", lambda v, g=g: v.tensor_tensor(
                gview(M1, g), gview(PEX, g), sbc(SGL + 48, g), ALU.is_ge))
        for g in range(2):
            p.emit("vector", lambda v, g=g: v.scalar_tensor_tensor(
                gview(T2, g), gview(M1, g), -BIG, gview(PEX, g),
                ALU.mult, ALU.add))
        for g in range(2):
            p.emit("vector", lambda v, g=g: v.reduce_max(
                sview(SGL + 64, g), gview(T2, g), AX.X))
        for g in range(2):
            p.emit("vector", lambda v, g=g: v.tensor_tensor(
                gview(T2, g), gview(T2, g), sbc(SGL + 64, g), ALU.is_ge))
        for g in range(2):
            p.emit("vector", lambda v, g=g: v.tensor_tensor(
                gview(M1, g), gview(M1, g), gview(T2, g), ALU.add))
        for g in range(2):
            p.emit("vector", lambda v, g=g: v.tensor_tensor(
                gview(M1, g), gview(M1, g), gview(PEX, g), ALU.mult))
        for g in range(2):
            p.emit("vector", lambda v, g=g: v.tensor_tensor(
                gview(CAL, g), gview(M1, g), sbc(SGL + 32, g), ALU.mult))
        p.emit("vector", lambda v: v.drain(fusable=False))
        tok_cb = p.emit("vector", lambda v: v.tensor_copy(
            cb8[:], XF[:, CAL:CAL + 128]), inc=(sv, 1))
        if b == 0:
            tap("sgl", XF[:, SGL:SGL + 80], 80, sv, tok_cb, f32=True)
            tap("m1", XF[:, M1:M1 + 128], 128, sv, tok_cb, f32=True)
        p.wait("tensor", sv, tok_cb)
        tok_gd = 0
        for g in range(2):
            p.bank_wait("tensor", "T")
            tok_tr = None
            for t_ in range(8):
                i = 8 * g + t_
                fn = lambda t, i=i, t_=t_: t.transpose(
                    psT[0:8, 128 * t_:128 * t_ + 128],
                    cb8[:, 8 * i:8 * i + 8], ident[:])
                tok_tr = p.emit("tensor", fn,
                                inc=(st, 1) if t_ == 7 else None)
            p.wait("vector", st, tok_tr)
            tok_gd = p.emit("vector", lambda v, g=g: v.tensor_copy(
                bct[0:8, 1024 * g:1024 * g + 1024], psT[0:8, 0:1024]),
                inc=(sv, 1))
            p.bank_set("T", sv, tok_gd)
        for i in range(NT):
            gate_done[i] = tok_gd

        hkeys = [("Alo", psA, 0), ("Ahi", psA, 512), ("Blo", psB, 0)]
        obanks = [[("Clo", psC, 0), ("Chi", psC, 512)],
                  [("D", psD, 0), ("Bhi", psB, 512)]]
        tok_wload = {}
        tok_expert_done = {}
        tok_g_read = {}

        def emit_wdma(k):
            j = k % 2
            p.wait("gpsimd", st, tok_expert_done.get(k - 2, attn_end_st))
            if k == 0:
                srcs = [(sw1d.ap(), W1BUF[j]), (sw2d.ap(), W2BUF[j])]
            else:
                srcs = [(w1d.ap()[k - 1], W1BUF[j]),
                        (w2d.ap()[k - 1], W2BUF[j])]
            tk = 0
            for src, dsto in srcs:
                s2 = src.rearrange("(c q) f -> q c f", q=128)
                d2 = AB[:, dsto:dsto + 8192].rearrange(
                    "q (c f) -> q c f", c=8)
                tk = p.emit("gpsimd", lambda g, s=s2, d=d2: g.dma_start(
                    d, s), inc=(swe[j], 16))
            tok_wload[k] = tk

        emit_wdma(0)
        emit_wdma(1)
        if b == 0:
            tap("w1c", AB[:, KF:KF + 8192], 8192, swe[0], tok_wload[0])
        tok_gelu = {}

        def emit_h(k, blk):
            j = k % 2
            p.wait("tensor", swe[j], tok_wload[k])
            gp = blk % 2
            gbase = GBUF[gp]
            p.wait("tensor", ss, xn2_ready[4 * blk + 3])
            gate_tok = tok_g_read.get(gp)
            tok_gl = None
            for ffc in range(ND):
                    key, ps, co = hkeys[ffc % 3]
                    p.bank_wait("tensor", key)
                    tok_mm = None
                    for dc in range(ND):
                        fn = lambda t, ps=ps, co=co, dc=dc, ffc=ffc, blk=blk, \
                            j=j: t.matmul(
                            ps[:, co:co + 512],
                            AB[:, W1BUF[j] + 1024 * dc + 128 * ffc:
                               W1BUF[j] + 1024 * dc + 128 * ffc + 128],
                            AB[:, XN + 2048 * dc + 512 * blk:
                               XN + 2048 * dc + 512 * blk + 512],
                            start=(dc == 0), stop=(dc == ND - 1))
                        tok_mm = p.emit("tensor", fn,
                                        inc=(st, 1) if dc == ND - 1 else None)
                    if (DBG == "h0" and b == 0 and k == 0 and blk == 0
                            and ffc == 0):
                        p.wait("vector", st, tok_mm)
                        tok_h0 = p.emit("vector", lambda v, ps=ps, co=co:
                                        v.tensor_copy(
                                            XF[:, TMP:TMP + 512],
                                            ps[:, co:co + 512]), inc=(sv, 1))
                        tap("h0", XF[:, TMP:TMP + 512], 512, sv, tok_h0,
                            f32=True)
                    p.wait("scalar", st, tok_mm)
                    if gate_tok is not None:
                        p.wait("scalar", st, gate_tok)
                    if k == 0:
                        bias = sb1f[:, ffc:ffc + 1]
                    else:
                        bias = b1f[:, 8 * (k - 1) + ffc:8 * (k - 1) + ffc + 1]
                    p.wait("scalar", sc, tok_bm)
                    tok_gl = p.emit("scalar", lambda s, gb=gbase, ffc=ffc,
                                    ps=ps, co=co, bias=bias: s.activation(
                                        AB[:, gb + 512 * ffc:
                                           gb + 512 * ffc + 512],
                                        ps[:, co:co + 512], AF.Gelu,
                                        bias=bias), inc=(ss, 1))
                    p.bank_set(key, ss, tok_gl)
            tok_gelu[(k, blk)] = tok_gl

        def emit_out(k, blk):
            j = k % 2
            gp = blk % 2
            gbase = GBUF[gp]
            tok_gl = tok_gelu[(k, blk)]
            p.wait("tensor", ss, tok_gl)
            if b == 0 and k == 0 and blk == 0:
                tap("g", AB[:, GBUF[0]:GBUF[0] + 4096], 4096, ss, tok_gl)
            for ii in range(4):
                    i_t = 4 * blk + ii
                    obk = obanks[ii % 2]
                    for okey, _, _ in obk:
                        p.bank_wait("tensor", okey)
                    tok_out = None
                    for ffc in range(ND):
                        for dd in range(2):
                            okey, ops_, oco = obk[dd]
                            last = (k != 0 and ffc == ND - 1 and dd == 1)
                            fn = lambda t, ops_=ops_, oco=oco, ffc=ffc, \
                                ii=ii, dd=dd, gb=gbase, j=j, last=last: \
                                t.matmul(
                                ops_[:, oco:oco + 512],
                                AB[:, gb + 512 * ffc + 128 * ii:
                                   gb + 512 * ffc + 128 * ii + 128],
                                AB[:, W2BUF[j] + 1024 * ffc + 512 * dd:
                                   W2BUF[j] + 1024 * ffc + 512 * dd + 512],
                                start=(ffc == 0), stop=last)
                            tok_out = p.emit("tensor", fn,
                                             inc=(st, 1) if last else None)
                    if k == 0:
                        p.wait("tensor", sv, gate_done[i_t])
                        p.wait("tensor", sc, tok_bm)
                        for dd in range(2):
                            okey, ops_, oco = obk[dd]
                            fnb = lambda t, ops_=ops_, oco=oco, i_t=i_t, \
                                dd=dd: t.matmul(
                                ops_[:, oco:oco + 512],
                                bct[:, 128 * i_t:128 * i_t + 128],
                                bmat[:, 512 * dd:512 * dd + 512],
                                start=False, stop=False)
                            p.emit("tensor", fnb)
                            fns = lambda t, ops_=ops_, oco=oco, dd=dd, \
                                last=(dd == 1): t.matmul(
                                ops_[:, oco:oco + 512],
                                ones1[:, 0:128],
                                bmatS[:, 512 * dd:512 * dd + 512],
                                start=False, stop=last)
                            tok_out = p.emit("tensor", fns,
                                             inc=(st, 1) if dd == 1 else None)
                    p.wait("vector", st, tok_out)
                    for dd in range(2):
                        okey, ops_, oco = obk[dd]
                        xs = XF[:, XR + 1024 * i_t + 512 * dd:
                                XR + 1024 * i_t + 512 * dd + 512]
                        if k == 0:
                            tok_e = p.emit(
                                "vector", lambda v, xs=xs, ops_=ops_,
                                oco=oco: v.tensor_add(
                                    xs, xs, ops_[:, oco:oco + 512]),
                                inc=(sv, 1))
                        else:
                            cs = XF[:, CAL + 8 * i_t + (k - 1):
                                    CAL + 8 * i_t + k]
                            tok_e = p.emit(
                                "vector", lambda v, xs=xs, ops_=ops_,
                                oco=oco, cs=cs: v.scalar_tensor_tensor(
                                    xs, ops_[:, oco:oco + 512], cs, xs,
                                    ALU.mult, ALU.add), inc=(sv, 1))
                        p.bank_set(okey, sv, tok_e)
                        if k == 8:
                            xfinal[i_t] = tok_e
            tok_g_read[gp] = st.n

        stages = [(k, blk) for k in range(9) for blk in range(4)]
        emit_h(*stages[0])
        for idx, (k, blk) in enumerate(stages):
            if idx + 1 < len(stages):
                emit_h(*stages[idx + 1])
            emit_out(k, blk)
            if blk == 3:
                tok_expert_done[k] = st.n
                if k + 2 <= 8:
                    emit_wdma(k + 2)
            if b == 0 and k == 0 and blk == 3:
                tap("xs0", XF[:, XR:XR + 8192], 8192, sv, sv.n, f32=True)

        store_toks = [0] * NT
        for i in range(NT):
            p.wait("sync", sv, xfinal[i])
            store_toks[i] = p.emit("sync", lambda s, b=b, i=i: s.dma_start(
                out_h[b, i], XF[:, XR + 1024 * i:XR + 1024 * i + 1024]),
                inc=(so, 16))
        return tok_expert_done[8], store_toks

    tok_bm, tok_gw = emit_constants()
    prev_end, prev_stores = None, None
    for b in range(NB):
        prev_end, prev_stores = emit_batch(b, prev_end, prev_stores,
                                           tok_bm, tok_gw)

    p.wait("sync", so, so.n)

    with nc.Block() as blk:
        @blk.sync
        def _(e):
            for fn in p.st["sync"]:
                fn(e)

        @blk.tensor
        def _(e):
            for fn in p.st["tensor"]:
                fn(e)

        @blk.vector
        def _(e):
            for fn in p.st["vector"]:
                fn(e)

        @blk.scalar
        def _(e):
            for fn in p.st["scalar"]:
                fn(e)

        @blk.gpsimd
        def _(e):
            for fn in p.st["gpsimd"]:
                fn(e)

    return nc


def kernel(**inputs):
    x = np.ascontiguousarray(inputs["L0_attn_x"])
    nc = build_program()
    b1 = np.asarray(inputs["L0_ffn_b1"])
    sb1 = np.asarray(inputs["L0_ffn_sb1"])
    gw = np.asarray(inputs["L0_ffn_gate_w"])
    base = {
        "wqt": np.ascontiguousarray(np.asarray(inputs["L0_attn_Wq"]).T),
        "wkt": np.ascontiguousarray(np.asarray(inputs["L0_attn_Wk"]).T),
        "wvt": np.ascontiguousarray(np.asarray(inputs["L0_attn_Wv"]).T),
        "wot": np.ascontiguousarray(np.asarray(inputs["L0_attn_Wo"]).T),
        "gwl": np.ascontiguousarray(
            gw.reshape(8, 128, 8).transpose(1, 0, 2).reshape(128, 64)),
        "gbv": np.ascontiguousarray(inputs["L0_ffn_gate_b"]),
        "sw1": np.ascontiguousarray(inputs["L0_ffn_sw1"]),
        "sb1l": np.ascontiguousarray(sb1.reshape(8, 128).T),
        "sw2": np.ascontiguousarray(inputs["L0_ffn_sw2"]),
        "sb2": np.ascontiguousarray(inputs["L0_ffn_sb2"]),
        "w1": np.ascontiguousarray(inputs["L0_ffn_w1"]),
        "b1l": np.ascontiguousarray(
            b1.reshape(8, 8, 128).transpose(2, 0, 1).reshape(128, 64)),
        "w2": np.ascontiguousarray(inputs["L0_ffn_w2"]),
        "b2": np.ascontiguousarray(inputs["L0_ffn_b2"]),
    }
    in_maps = []
    for c in range(8):
        m = dict(base)
        m["x"] = np.ascontiguousarray(x[NB * c:NB * c + NB])
        in_maps.append(m)
    res = run_bass_kernel_spmd(nc, in_maps, core_ids=list(range(8)))
    if os.environ.get("KDBG", ""):
        kernel.dbg = (res.results[0].get("dbgb"), res.results[0].get("dbgf"))
    return np.concatenate([r["out"] for r in res.results], axis=0)


if __name__ == "__main__":
    import time
    t0 = time.time()
    nc = build_program()
    print("built ok in %.1fs" % (time.time() - t0))
